# version 14
# baseline (speedup 1.0000x reference)
import numpy as np
from contextlib import ExitStack
import concourse.bass as bass
import concourse.mybir as mybir
from concourse.bass_utils import run_bass_kernel_spmd
from concourse.bass_core_rust import IndirectOffsetOnAxis

F32 = mybir.dt.float32
BF16 = mybir.dt.bfloat16
AF = mybir.ActivationFunctionType
ALU = mybir.AluOpType
AX = mybir.AxisListType

T = 2048
D = 1024
NTC = T // 128
MASKV = -1e9


def _bf16(a):
    import ml_dtypes
    return np.asarray(a).astype(ml_dtypes.bfloat16)


def host_consts():
    ident = np.eye(128, dtype=np.float32)
    g = np.arange(896)[None, :]
    pp = np.arange(128)[:, None]
    band = np.where(g >= pp + 384, 0.0, MASKV).astype(np.float32)
    ut = (np.arange(128)[:, None] < np.arange(128)[None, :]).astype(np.float32)
    iota = np.arange(128, dtype=np.float32).reshape(128, 1)
    erow = (np.arange(16, dtype=np.float32) * 2048).reshape(1, 16)
    iota16 = (np.arange(16)[None, :] * 128 + np.arange(128)[:, None]).astype(np.float32)
    return {"ident_f": ident, "ident_b": _bf16(ident), "band_b": _bf16(band),
            "ut": ut, "iota": iota, "erow": erow, "iota16": iota16}


class Prog:
    def __init__(self, name="prog"):
        self.nc = bass.Bass("TRN2", target_bir_lowering=False)
        self.es = ExitStack()
        self.es.__enter__()
        self._sems = {}
        self.c = {}
        self.inputs = {}
        self.sharded = {}
        self.outputs = []
        self.block = None

    def din(self, name, arr, sharded=False):
        arr0 = arr[0] if sharded else arr
        dt = BF16 if arr0.dtype.name == "bfloat16" else F32
        ap = self.nc.dram_tensor(name, list(arr0.shape), dt, kind="ExternalInput").ap()
        if sharded:
            self.sharded[name] = arr
        else:
            self.inputs[name] = np.ascontiguousarray(arr0)
        return ap

    def dout(self, name, shape, dt=F32):
        self.outputs.append(name)
        return self.nc.dram_tensor(name, list(shape), dt, kind="ExternalOutput").ap()

    def sb(self, name, shape, dt):
        return self.es.enter_context(self.nc.sbuf_tensor(name, shape, dt))

    def psum(self, name, shape, dt=F32):
        return self.es.enter_context(self.nc.psum_tensor(name, shape, dt))

    def sem(self, name):
        if name not in self._sems:
            self._sems[name] = self.es.enter_context(self.nc.semaphore(name))
            self.c[name] = 0
        return self._sems[name]

    def inc(self, inst, name, k=1):
        inst.then_inc(self._sems[name], k)
        self.c[name] += k
        return self.c[name]

    def n(self, name):
        return self.c[name]

    def wait(self, eng, name, val=None):
        eng.wait_ge(self._sems[name], self.c[name] if val is None else val)

    def vstep(self, v, inst):
        self.inc(inst, "VE")
        v.wait_ge(self._sems["VE"], self.c["VE"])

    def sstep(self, s, inst):
        self.inc(inst, "SC")
        s.wait_ge(self._sems["SC"], self.c["SC"])

    def vinc(self, inst):
        return self.inc(inst, "VE")

    def xadd(self, v, idx, emit_fn):
        if self.xw[idx]:
            v.wait_ge(self._sems["VE"], self.xw[idx])
        self.vinc(emit_fn())
        self.xw[idx] = self.c["VE"]

    def sinc(self, inst):
        return self.inc(inst, "SC")

    def start(self):
        for s in ("LD", "LI", "LC", "LX", "MM", "SC", "VE", "GF", "OUT"):
            self.sem(s)
        self.block = self.es.enter_context(self.nc.Block())

    def done(self):
        self.es.__exit__(None, None, None)

    def run(self, n_cores=8, **kw):
        in_maps = []
        for ci in range(n_cores):
            m = dict(self.inputs)
            for k, v in self.sharded.items():
                m[k] = v[ci]
            in_maps.append(m)
        return run_bass_kernel_spmd(self.nc, in_maps, core_ids=list(range(n_cores)), **kw)



def emit_consts(p, cd):
    p.ident_f = p.sb("ident_f", [128, 128], F32)
    p.ident_b = p.sb("ident_b", [128, 128], BF16)
    p.band_b = p.sb("band_b", [128, 896], BF16)
    p.ones_rb = p.sb("ones_rb", [1, 512], BF16)
    p.ones_cb = p.sb("ones_cb", [128, 1], BF16)
    p.ones_rf = p.sb("ones_rf", [1, 128], F32)
    p.ones_rbig = p.sb("ones_rbig", [1, 128], BF16)
    p.eps_col = p.sb("eps_col", [128, 1], F32)
    p.ones_cf = p.sb("ones_cf", [128, 1], F32)
    p.ut_sb = p.sb("ut_sb", [128, 128], F32)
    p.iota_sb = p.sb("iota_sb", [128, 1], F32)
    p.erow_sb = p.sb("erow_sb", [1, 16], F32)
    p.iota16_sb = p.sb("iota16_sb", [128, 16], F32)
    d_if = p.din("c_ident_f", cd["ident_f"])
    d_ib = p.din("c_ident_b", cd["ident_b"])
    d_bb = p.din("c_band_b", cd["band_b"])
    d_ut = p.din("c_ut", cd["ut"])
    d_io = p.din("c_iota", cd["iota"])
    d_er = p.din("c_erow", cd["erow"])
    d_i16 = p.din("c_iota16", cd["iota16"])
    blk = p.block

    @blk.sync
    def _(y):
        p.inc(y.dma_start(out=p.ident_f[:], in_=d_if[:]), "LC", 16)
        p.inc(y.dma_start(out=p.ident_b[:], in_=d_ib[:]), "LC", 16)
        p.inc(y.dma_start(out=p.band_b[:], in_=d_bb[:]), "LC", 16)
        p.inc(y.dma_start(out=p.ut_sb[:], in_=d_ut[:]), "LC", 16)
        p.inc(y.dma_start(out=p.iota_sb[:], in_=d_io[:]), "LC", 16)
        p.inc(y.dma_start(out=p.erow_sb[:], in_=d_er[:]), "LC", 16)
        p.inc(y.dma_start(out=p.iota16_sb[:], in_=d_i16[:]), "LC", 16)

    @blk.vector
    def _(v):
        p.vstep(v, v.memset(p.ones_rb[:], 1.0))
        p.vstep(v, v.memset(p.ones_cb[:], 1.0))
        p.vstep(v, v.memset(p.ones_rf[:], 1.0))
        p.vstep(v, v.memset(p.ones_rbig[:], 1.0))
        p.vstep(v, v.memset(p.eps_col[:], 1e-5))
        p.vstep(v, v.memset(p.ones_cf[:], 1.0))
    p.consts_lc = p.n("LC")
    p.consts_ve = p.n("VE")


def emit_load_x(p, d_x):
    p.x_tm = p.sb("x_tm", [128, NTC * 1024], F32)
    blk = p.block

    @blk.sync
    def _(y):
        src = d_x.rearrange("(j p) d -> p j d", p=128)
        dst = p.x_tm[:].rearrange("p (j d) -> p j d", j=NTC)
        p.inc(y.dma_start(out=dst, in_=src), "LX", 16)
        y.wait_ge(p._sems["LX"], p.c["LX"])
    p.x_li = p.n("LX")
    p.x_ready = [("LX", p.x_li)]
    p.xw = [0] * 32


def emit_store_x(p, d_y, src_sb):
    blk = p.block

    @blk.sync
    def _(y):
        for k in ("VE", "SC", "MM", "LD", "LI", "LC", "LX"):
            if p.c[k]:
                p.wait(y, k)
        dst = d_y.rearrange("(j p) d -> p j d", p=128)
        src = src_sb[:].rearrange("p (j d) -> p j d", j=NTC)
        p.inc(y.dma_start(out=dst, in_=src), "OUT", 16)
        p.wait(y, "OUT")


def emit_barrier(p):
    blk = p.block
    snap = {k: p.c[k] for k in ("LD", "LI", "LC", "LX", "MM", "SC", "VE", "GF")}

    def waits(e):
        for k, v in snap.items():
            if v:
                e.wait_ge(p._sems[k], v)

    @blk.tensor
    def _(t):
        waits(t)

    @blk.scalar
    def _(s):
        waits(s)

    @blk.vector
    def _(v):
        waits(v)

    @blk.gpsimd
    def _(g):
        waits(g)



def alloc_ln(p):
    p.xn_fm = p.sb("xn_fm", [128, 8 * 2048], BF16)
    p.xsq = af32(p, 77824, 1024)
    p.xnf = af32(p, 81920, 1024)
    p.xnT = af32(p, 86016, 1024)
    p.xnb = p.sb("xnb", [128, 1024], BF16)
    p.negm8 = p.sb("negm8", [128, 16], F32)
    p.maxall = p.sb("maxall", [128, 16], F32)
    p.m2sb = p.sb("m2sb", [1, 2], F32)
    p.m2bf = p.sb("m2bf", [1, 1], BF16)
    p.lncol = p.sb("lncol", [128, 8], F32)
    p.rexp = p.sb("rexp", [128, 16], F32)
    p.rsel = p.sb("rsel", [128, 16], F32)
    p.rsel2 = p.sb("rsel2", [128, 16], F32)
    p.rtmp = p.sb("rtmp", [128, 16], F32)
    p.rcol = p.sb("rcol", [128, 4], F32)
    p.gw_sb = p.sb("gw_sb", [128, 8 * 16], F32)
    p.rb_sb = p.sb("rb_sb", [1, 16], F32)
    p.C_tm = {}


def emit_ln_router(p, tag, E, K, d_gwT, d_rb, sparse=False):
    blk = p.block
    if sparse:
        p.sel_all = p.sb("sel_all_" + tag, [128, NTC * E], F32)
        p.xn_hbm = p.nc.dram_tensor("xn_hbm_" + tag, [T, D], BF16, kind="Internal").ap()
        p.C_hbm = p.nc.dram_tensor("C_hbm_" + tag, [T * E, 1], F32, kind="Internal").ap()
    p.C_tm[tag] = p.sb("C_tm_" + tag, [128, NTC * E], F32)
    C_tm = p.C_tm[tag]
    if tag == "L1":
        p.CT_sb = p.sb("CT_sb", [16, 2048], BF16)
        p.cbf = p.sb("cbf", [128, 16], BF16)
    s1 = p.lncol[:, 0:1]; s2 = p.lncol[:, 1:2]; mean = p.lncol[:, 2:3]
    ex2 = p.lncol[:, 3:4]; var = p.lncol[:, 4:5]; rstd = p.lncol[:, 5:6]
    negmr = p.lncol[:, 6:7]
    m_ = p.rcol[:, 0:1]; negm_ = p.rcol[:, 1:2]; sum_ = p.rcol[:, 2:3]; rec_ = p.rcol[:, 3:4]

    @blk.sync
    def _(y):
        if p.c["MM"] > 0:
            p.wait(y, "MM")
        src = d_gwT.rearrange("(c p) e -> p c e", p=128)
        dst = p.gw_sb[:, 0:8 * E].rearrange("p (c e) -> p c e", c=8)
        p.inc(y.dma_start(out=dst, in_=src), "LI", 16)
        if d_rb is not None:
            p.inc(y.dma_start(out=p.rb_sb[:, 0:E], in_=d_rb), "LI", 16)
        y.wait_ge(p._sems["LI"], p.c["LI"])
    li_w = p.n("LI")

    with p.nc.psum_tensor("psT_b_" + tag, [128, 1024], BF16) as psT_b, \
         p.nc.psum_tensor("psT_f_" + tag, [128, 1024], F32) as psT_f, \
         p.nc.psum_tensor("ps_r_" + tag, [128, 16], F32) as ps_r:
        for tc in range(NTC):
            x_c = p.x_tm[:, tc * 1024:(tc + 1) * 1024]

            @blk.vector
            def _(v):
                for sk, sv in p.x_ready:
                    v.wait_ge(p._sems[sk], sv)
                p.vstep(v, v.reduce_sum(s1, x_c, axis=AX.X))
            ve_s1 = p.n("VE")

            @blk.scalar
            def _(s):
                for sk, sv in p.x_ready:
                    s.wait_ge(p._sems[sk], sv)
                if tc > 0:
                    s.wait_ge(p._sems["VE"], p.ln_ve_done)
                p.sstep(s, s.activation(p.xsq[:], x_c, AF.Square, accum_out=s2))
            sc_s2 = p.n("SC")

            @blk.vector
            def _(v):
                v.wait_ge(p._sems["VE"], ve_s1)
                v.wait_ge(p._sems["SC"], sc_s2)
                p.vstep(v, v.tensor_scalar(mean, s1, 1.0 / D, None, ALU.mult))
                p.vstep(v, v.tensor_scalar(ex2, s2, 1.0 / D, None, ALU.mult))
                p.vstep(v, v.scalar_tensor_tensor(var, mean, -1.0, mean, ALU.mult, ALU.mult))
                p.vstep(v, v.tensor_add(var, var, ex2))
            ve_var = p.n("VE")

            @blk.scalar
            def _(s):
                s.wait_ge(p._sems["VE"], ve_var)
                p.sstep(s, s.activation(rstd, var, AF.Ln, bias=p.eps_col[:]))
                p.sstep(s, s.activation(rstd, rstd, AF.Exp, scale=-0.5))
            sc_sq = p.n("SC")

            @blk.vector
            def _(v):
                v.wait_ge(p._sems["SC"], sc_sq)
                p.vstep(v, v.scalar_tensor_tensor(negmr, mean, -1.0, rstd, ALU.mult, ALU.mult))
            ve_nm = p.n("VE")

            @blk.scalar
            def _(s):
                s.wait_ge(p._sems["VE"], ve_nm)
                if tc > 0:
                    s.wait_ge(p._sems["MM"], p.ln_mm_tp)
                if sparse and tc > 0:
                    s.wait_ge(p._sems["LI"], p.ln_xns)
                p.sstep(s, s.activation(p.xnb[:], x_c, AF.Identity, bias=negmr, scale=rstd))
                p.sstep(s, s.activation(p.xnf[:], x_c, AF.Identity, bias=negmr, scale=rstd))
            sc_xn = p.n("SC")

            if sparse:
                @blk.sync
                def _(y):
                    y.wait_ge(p._sems["SC"], sc_xn)
                    p.inc(y.dma_start(
                        out=p.xn_hbm[tc * 128:(tc + 1) * 128, :], in_=p.xnb[:]), "LI", 16)
                    y.wait_ge(p._sems["LI"], p.c["LI"])
                p.ln_xns = p.n("LI")

            @blk.tensor
            def _(t):
                t.wait_ge(p._sems["SC"], sc_xn)
                t.wait_ge(p._sems["LC"], p.consts_lc)
                if tc > 0:
                    t.wait_ge(p._sems["VE"], p.ln_ve_evac)
                for b in range(8):
                    i1 = t.transpose(psT_b[:, b * 128:(b + 1) * 128],
                                     p.xnb[:, b * 128:(b + 1) * 128], p.ident_b[:])
                p.inc(i1, "MM")
                for b in range(8):
                    i2 = t.transpose(psT_f[:, b * 128:(b + 1) * 128],
                                     p.xnf[:, b * 128:(b + 1) * 128], p.ident_f[:])
                p.inc(i2, "MM")
            p.ln_mm_tp = p.n("MM")
            mm_tp = p.n("MM")

            @blk.vector
            def _(v):
                v.wait_ge(p._sems["MM"], mm_tp)
                dst = p.xn_fm[:].rearrange("p (c n) -> p c n", c=8)[:, :, tc * 128:(tc + 1) * 128]
                p.vstep(v, v.tensor_copy(dst, psT_b[:].rearrange("p (c n) -> p c n", c=8)))
                p.vstep(v, v.tensor_copy(p.xnT[:], psT_f[:]))
            p.ln_ve_evac = p.n("VE")
            ve_ev = p.n("VE")

            @blk.tensor
            def _(t):
                t.wait_ge(p._sems["VE"], ve_ev)
                t.wait_ge(p._sems["LI"], li_w)
                if tc > 0:
                    t.wait_ge(p._sems["SC"], p.ln_sc_rexp)
                for b in range(8):
                    i1 = t.matmul(ps_r[:, 0:E], p.xnT[:, b * 128:(b + 1) * 128],
                                  p.gw_sb[:, b * E:(b + 1) * E], start=(b == 0),
                                  stop=(b == 7 and d_rb is None))
                if d_rb is not None:
                    i1 = t.matmul(ps_r[:, 0:E], p.ones_rf[:], p.rb_sb[:, 0:E],
                                  start=False, stop=True)
                p.inc(i1, "MM")
            mm_r = p.n("MM")

            @blk.vector
            def _(v):
                v.wait_ge(p._sems["MM"], mm_r)
                p.vstep(v, v.reduce_max(m_, ps_r[:, 0:E], axis=AX.X))
                p.vstep(v, v.tensor_scalar(negm_, m_, -1.0, None, ALU.mult))
            ve_m = p.n("VE")

            @blk.scalar
            def _(s):
                s.wait_ge(p._sems["VE"], ve_m)
                p.sstep(s, s.activation(p.rexp[:, 0:E], ps_r[:, 0:E], AF.Exp, bias=negm_))
            p.ln_sc_rexp = p.n("SC")
            sc_e = p.n("SC")

            @blk.vector
            def _(v):
                v.wait_ge(p._sems["SC"], sc_e)
                cur = p.rtmp[:, 0:E]
                p.vstep(v, v.tensor_copy(cur, p.rexp[:, 0:E]))
                p.vstep(v, v.memset(p.rsel[:, 0:E], 0.0))
                for j in range(K):
                    p.vstep(v, v.reduce_max(m_, cur, axis=AX.X))
                    p.vstep(v, v.tensor_scalar(p.rsel2[:, 0:E], cur, m_, None, ALU.is_equal))
                    p.vstep(v, v.tensor_add(p.rsel[:, 0:E], p.rsel[:, 0:E], p.rsel2[:, 0:E]))
                    if j < K - 1:
                        p.vstep(v, v.tensor_scalar(p.rsel2[:, 0:E], p.rsel2[:, 0:E],
                                                   -1.0, 1.0, ALU.mult, ALU.add))
                        p.vstep(v, v.tensor_mul(cur, cur, p.rsel2[:, 0:E]))
                if sparse:
                    p.vstep(v, v.tensor_copy(p.sel_all[:, tc * E:(tc + 1) * E],
                                             p.rsel[:, 0:E]))
                p.vstep(v, v.tensor_mul(p.rtmp[:, 0:E], p.rexp[:, 0:E], p.rsel[:, 0:E]))
                p.vstep(v, v.reduce_sum(sum_, p.rtmp[:, 0:E], axis=AX.X))
                p.vstep(v, v.reciprocal(rec_, sum_))
                p.vstep(v, v.tensor_scalar(C_tm[:, tc * E:(tc + 1) * E], p.rtmp[:, 0:E],
                                           rec_, None, ALU.mult))
                if tag == "L1":
                    p.vstep(v, v.tensor_copy(p.cbf[:, 0:E], C_tm[:, tc * E:(tc + 1) * E]))
            ve_c = p.n("VE")

            if tag == "L1":
                @blk.tensor
                def _(t):
                    t.wait_ge(p._sems["VE"], ve_c)
                    p.inc(t.transpose(psT_b[0:16, 0:128], p.cbf[:, 0:16], p.ident_b[:]), "MM")
                mm_ct = p.n("MM")

                @blk.vector
                def _(v):
                    v.wait_ge(p._sems["MM"], mm_ct)
                    p.vstep(v, v.tensor_copy(p.CT_sb[0:16, tc * 128:(tc + 1) * 128],
                                             psT_b[0:16, 0:128]))
                p.ln_ve_evac = p.n("VE")
            p.ln_ve_done = p.n("VE")
    if sparse:
        @blk.sync
        def _(y):
            p.wait(y, "VE")
            sc_ap = p.C_tm[tag][:].rearrange("p (j e) -> p j e", j=NTC)
            dst = p.C_hbm.rearrange("(j p e) o -> p j (e o)", j=NTC, p=128)
            p.inc(y.dma_start(out=dst, in_=sc_ap), "LI", 16)
            y.wait_ge(p._sems["LI"], p.c["LI"])
        p.ln_cs = p.n("LI")
    p.xn_ve = p.n("VE")
    p.xn_mm = p.n("MM")



ARENA_B = 88 * 1024


def alloc_arena(p):
    p.arena = p.sb("arena", [128, ARENA_B // 2], BF16)


def abf(p, off_b, n, parts=128):
    return p.arena[0:parts, off_b // 2: off_b // 2 + n]


def af32(p, off_b, n, parts=128):
    return p.arena[:].bitcast(F32)[0:parts, off_b // 4: off_b // 4 + n]


class SegStream:

    def __init__(self, p, name, slot_views, srcs):
        self.p, self.name = p, name
        self.slots = slot_views
        self.srcs = srcs
        self.ld = {}

    def _load(self, g, seg, gate_mm, gate_all=False):
        p = self.p
        dst, src = self.srcs[seg](self.slots[seg % len(self.slots)])
        self.ld[seg] = _gload(p, g, [(dst, src)], gate_mm=gate_mm, gate_all=gate_all)

    def prime(self, gate_mm=0):
        p, blk = self.p, self.p.block

        @blk.gpsimd
        def _(g):
            for s in range(min(len(self.slots), len(self.srcs))):
                self._load(g, s, gate_mm, gate_all=True)

    def done(self, seg, mm_count):
        nxt = seg + len(self.slots)
        if nxt < len(self.srcs):
            p, blk = self.p, self.p.block

            @blk.gpsimd
            def _(g):
                self._load(g, nxt, mm_count)

    def use(self, seg):
        return self.slots[seg % len(self.slots)], self.ld[seg]


def _gload(p, g, dmas, gate_mm=0, gate_all=False):
    if gate_mm > 0:
        g.wait_ge(p._sems["MM"], gate_mm)
    if gate_all:
        for k in ("SC", "VE"):
            if p.c[k]:
                g.wait_ge(p._sems[k], p.c[k])
    for dst, src in dmas:
        p.inc(g.dma_start(out=dst, in_=src), "LD", 16)
    cum = p.n("LD")
    g.wait_ge(p._sems["LD"], cum)
    return cum


def _r3(ap, c):
    return ap.rearrange("p (c n) -> p c n", c=c)



def emit_ffn_l1(p, d_w1, d_b1fm, d_w2, d_b2):
    blk = p.block
    E = 16
    NT = 1024
    b1_sb = af32(p, 88064, 16 * E)
    b2_sb = abf(p, 86016, 1024, parts=16)
    w1slots = [abf(p, 0, 4096), abf(p, 8192, 4096)]
    w2slots = [abf(p, 16384, 8192), abf(p, 32768, 8192)]
    hmid = abf(p, 49152, 16 * NT)
    ctx = ExitStack()
    ph = ctx.enter_context(p.nc.psum_tensor("ph_l1", [128, 2048], F32))
    po = ctx.enter_context(p.nc.psum_tensor("po_l1", [128, 1024], F32))
    pcb = ctx.enter_context(p.nc.psum_tensor("pcb_l1", [128, 512], F32))

    @blk.sync
    def _(y):
        p.wait(y, "MM")
        p.inc(y.dma_start(out=b1_sb[:], in_=d_b1fm[:]), "LI", 16)
        p.wait(y, "LI")
    li_b1 = p.n("LI")

    @blk.gpsimd
    def _(g):
        _gload(p, g, [(b2_sb[:], d_b2[:])], gate_mm=p.n("MM"), gate_all=True)
    ld_b2 = p.n("LD")

    seq = [(tt, e) for tt in range(2) for e in range(E)]

    def w1src(tt, e, q):
        def f(slot):
            return (_r3(slot, 8), d_w1[e, :, q])
        return f

    def w2src(tt, e, h):
        def f(slot):
            return (_r3(slot, 16), d_w2[e, :, h])
        return f

    s_w1 = SegStream(p, "w1", w1slots, [w1src(tt, e, q) for tt, e in seq for q in range(4)])
    s_w2 = SegStream(p, "w2", w2slots, [w2src(tt, e, h) for tt, e in seq for h in range(2)])
    s_w1.prime(gate_mm=p.n("MM"))
    s_w2.prime(gate_mm=p.n("MM"))

    cb_prev = 0
    for tc in range(NTC):
        for h in range(2):
            @blk.tensor
            def _(t):
                t.wait_ge(p._sems["VE"], p.xn_ve)
                t.wait_ge(p._sems["LD"], ld_b2)
                if cb_prev:
                    t.wait_ge(p._sems["VE"], cb_prev)
                p.inc(t.matmul(pcb[:], p.CT_sb[0:16, tc * 128:(tc + 1) * 128],
                               b2_sb[0:16, h * 512:(h + 1) * 512],
                               start=True, stop=True), "MM")
            mm_cb = p.n("MM")

            @blk.vector
            def _(v):
                v.wait_ge(p._sems["MM"], mm_cb)
                xs = p.x_tm[:, tc * 1024 + h * 512: tc * 1024 + (h + 1) * 512]
                p.xadd(v, tc * 2 + h, lambda: v.tensor_add(xs, xs, pcb[:]))
            cb_prev = p.n("VE")

    gel = {}
    pove = {}
    jg = 0
    npo = 0
    for i, (tt, e) in enumerate(seq):
        prev_b_mm = p.n("MM")
        jg0 = jg
        sc0 = p.n("SC")
        mm_as = {}
        for ic in range(16):
            q = ic // 4
            w1buf, w1ld = s_w1.use(i * 4 + q)

            @blk.tensor
            def _(t):
                t.wait_ge(p._sems["LD"], w1ld)
                if i == 0 and ic == 0:
                    t.wait_ge(p._sems["VE"], p.xn_ve)
                if jg >= 2:
                    t.wait_ge(p._sems["SC"], gel[jg - 2])
                for n in range(2):
                    for dc in range(8):
                        i1 = t.matmul(ph[:, (jg % 2) * 1024 + n * 512:(jg % 2) * 1024 + n * 512 + 512],
                                      w1buf[:, dc * 512 + (ic % 4) * 128: dc * 512 + (ic % 4) * 128 + 128],
                                      p.xn_fm[:, dc * 2048 + tt * NT + n * 512: dc * 2048 + tt * NT + n * 512 + 512],
                                      start=(dc == 0), stop=(dc == 7))
                p.inc(i1, "MM")
            mm_as[ic] = p.n("MM")
            gel[jg] = sc0 + ic + 1
            jg += 1
            if ic % 4 == 3:
                s_w1.done(i * 4 + q, mm_as[ic])

        @blk.scalar
        def _(s):
            for ic in range(16):
                s.wait_ge(p._sems["MM"], mm_as[ic])
                if ic == 0:
                    s.wait_ge(p._sems["MM"], prev_b_mm)
                    s.wait_ge(p._sems["LI"], li_b1)
                p.sinc(s.activation(hmid[:, ic * NT:(ic + 1) * NT],
                                    ph[:, ((jg0 + ic) % 2) * 1024:((jg0 + ic) % 2) * 1024 + 1024],
                                    AF.Gelu, bias=b1_sb[:, e * 16 + ic: e * 16 + ic + 1]))
        assert p.n("SC") == sc0 + 16
        sc_a_done = p.n("SC")

        for h in range(2):
            w2buf, w2ld = s_w2.use(i * 2 + h)
            for ts in range(8):
                @blk.tensor
                def _(t):
                    t.wait_ge(p._sems["LD"], w2ld)
                    if h == 0 and ts == 0:
                        t.wait_ge(p._sems["SC"], sc_a_done)
                    if npo >= 2:
                        t.wait_ge(p._sems["VE"], pove[npo - 2])
                    for ic in range(16):
                        i1 = t.matmul(po[:, (npo % 2) * 512:(npo % 2) * 512 + 512],
                                      hmid[:, ic * NT + ts * 128: ic * NT + ts * 128 + 128],
                                      w2buf[:, ic * 512:(ic + 1) * 512],
                                      start=(ic == 0), stop=(ic == 15))
                    p.inc(i1, "MM")
                mm_b = p.n("MM")

                @blk.vector
                def _(v):
                    v.wait_ge(p._sems["MM"], mm_b)
                    tc = tt * 8 + ts
                    c_col = p.C_tm["L1"][:, tc * 16 + e: tc * 16 + e + 1]
                    xs = p.x_tm[:, tc * 1024 + h * 512: tc * 1024 + h * 512 + 512]
                    po_s = po[:, (npo % 2) * 512:(npo % 2) * 512 + 512]
                    p.xadd(v, tc * 2 + h, lambda: v.scalar_tensor_tensor(
                        xs, po_s, c_col, xs, ALU.mult, ALU.add))
                pove[npo] = p.n("VE")
                npo += 1
            s_w2.done(i * 2 + h, p.n("MM"))
    p.ffn1_mm = p.n("MM")
    ctx.close()


def emit_ffn_l0(p, d_gup, d_dp):
    blk = p.block
    E = 8
    NT = 1024
    gu_slots = [abf(p, 0, 4096), abf(p, 8192, 4096),
                abf(p, 16384, 4096), abf(p, 24576, 4096)]
    dn_slots = [abf(p, 32768, 4096), abf(p, 40960, 4096)]
    act = abf(p, 49152, 8 * NT)
    sg = [abf(p, 65536, NT), abf(p, 67584, NT)]
    ctx = ExitStack()
    phg = ctx.enter_context(p.nc.psum_tensor("phg_l0", [128, 2048], F32))
    phu = ctx.enter_context(p.nc.psum_tensor("phu_l0", [128, 1024], F32))
    po = ctx.enter_context(p.nc.psum_tensor("po_l0", [128, 1024], F32))

    seq = [(tt, e) for tt in range(2) for e in range(E)]

    def gusrc(e, q):
        def f(slot):
            return (_r3(slot, 8), d_gup[e, :, q])
        return f

    def dnsrc(e, h):
        def f(slot):
            return (_r3(slot, 8), d_dp[e, :, h])
        return f

    s_gu = SegStream(p, "gu", gu_slots, [gusrc(e, q) for tt, e in seq for q in range(4)])
    s_dn = SegStream(p, "dn", dn_slots, [dnsrc(e, h) for tt, e in seq for h in range(2)])
    s_gu.prime(gate_mm=p.n("MM"))
    s_dn.prime(gate_mm=p.n("MM"))

    sil = {}
    mul = {}
    pove = {}
    jg = 0
    npo = 0
    for i, (tt, e) in enumerate(seq):
        prev_b_mm = p.n("MM")
        jg0 = jg
        sil0 = p.n("SC")
        mul0 = p.n("VE")
        mm_gs = {}
        mm_us = {}
        for icq in range(8):
            gseg = i * 4 + (icq // 4) * 2
            useg = gseg + 1
            gbuf, gld = s_gu.use(gseg)
            ubuf, uld = s_gu.use(useg)

            @blk.tensor
            def _(t):
                t.wait_ge(p._sems["LD"], gld)
                if i == 0 and icq == 0:
                    t.wait_ge(p._sems["VE"], p.xn_ve)
                if jg >= 2:
                    t.wait_ge(p._sems["SC"], sil[jg - 2])
                for dc in range(8):
                    for n in range(2):
                        i1 = t.matmul(phg[:, (jg % 2) * 1024 + n * 512:(jg % 2) * 1024 + n * 512 + 512],
                                      gbuf[:, dc * 512 + (icq % 4) * 128: dc * 512 + (icq % 4) * 128 + 128],
                                      p.xn_fm[:, dc * 2048 + tt * NT + n * 512: dc * 2048 + tt * NT + n * 512 + 512],
                                      start=(dc == 0), stop=(dc == 7))
                p.inc(i1, "MM")
                t.wait_ge(p._sems["LD"], uld)
                if jg >= 1:
                    t.wait_ge(p._sems["VE"], mul[jg - 1])
                for dc in range(8):
                    for n in range(2):
                        i1 = t.matmul(phu[:, n * 512:(n + 1) * 512],
                                      ubuf[:, dc * 512 + (icq % 4) * 128: dc * 512 + (icq % 4) * 128 + 128],
                                      p.xn_fm[:, dc * 2048 + tt * NT + n * 512: dc * 2048 + tt * NT + n * 512 + 512],
                                      start=(dc == 0), stop=(dc == 7))
                p.inc(i1, "MM")
            mm_gs[icq] = p.n("MM") - 1
            mm_us[icq] = p.n("MM")
            if icq == 3 or icq == 7:
                s_gu.done(gseg, p.n("MM"))
                s_gu.done(useg, p.n("MM"))
            sil[jg] = sil0 + icq + 1
            mul[jg] = mul0 + icq + 1
            jg += 1

        @blk.scalar
        def _(s):
            for icq in range(8):
                j = jg0 + icq
                s.wait_ge(p._sems["MM"], mm_gs[icq])
                if j >= 2:
                    s.wait_ge(p._sems["VE"], mul[j - 2])
                p.sinc(s.activation(sg[j % 2][:], phg[:, (j % 2) * 1024:(j % 2) * 1024 + 1024],
                                    AF.Silu))
        assert p.n("SC") == sil0 + 8

        @blk.vector
        def _(v):
            for icq in range(8):
                j = jg0 + icq
                v.wait_ge(p._sems["SC"], sil[j])
                v.wait_ge(p._sems["MM"], mm_us[icq])
                if icq == 0:
                    v.wait_ge(p._sems["MM"], prev_b_mm)
                p.vinc(v.tensor_mul(act[:, icq * NT:(icq + 1) * NT], sg[j % 2][:], phu[:]))
        assert p.n("VE") == mul0 + 8
        ve_a_done = p.n("VE")

        for h in range(2):
            dbuf, dld = s_dn.use(i * 2 + h)
            for ts in range(8):
                @blk.tensor
                def _(t):
                    t.wait_ge(p._sems["LD"], dld)
                    if h == 0 and ts == 0:
                        t.wait_ge(p._sems["VE"], ve_a_done)
                    if npo >= 2:
                        t.wait_ge(p._sems["VE"], pove[npo - 2])
                    for qc in range(8):
                        i1 = t.matmul(po[:, (npo % 2) * 512:(npo % 2) * 512 + 512],
                                      act[:, qc * NT + ts * 128: qc * NT + ts * 128 + 128],
                                      dbuf[:, qc * 512:(qc + 1) * 512],
                                      start=(qc == 0), stop=(qc == 7))
                    p.inc(i1, "MM")
                mm_b = p.n("MM")

                @blk.vector
                def _(v):
                    v.wait_ge(p._sems["MM"], mm_b)
                    tc = tt * 8 + ts
                    c_col = p.C_tm["L0"][:, tc * 8 + e: tc * 8 + e + 1]
                    xs = p.x_tm[:, tc * 1024 + h * 512: tc * 1024 + h * 512 + 512]
                    po_s = po[:, (npo % 2) * 512:(npo % 2) * 512 + 512]
                    p.xadd(v, tc * 2 + h, lambda: v.scalar_tensor_tensor(
                        xs, po_s, c_col, xs, ALU.mult, ALU.add))
                pove[npo] = p.n("VE")
                npo += 1
            s_dn.done(i * 2 + h, p.n("MM"))
    p.ffn0_mm = p.n("MM")
    ctx.close()



def _norm_bound(p, ps_misc, qsq_src_q, qsq_src_k, qsq, h, n_heads_done_mm):
    blk = p.block
    qn = ps_misc[:, 0:32]
    tr = ps_misc[0:1, 64:192]
    src_ve = p.n("VE")

    @blk.vector
    def _(v):
        v.wait_ge(p._sems["MM"], n_heads_done_mm)
        v.wait_ge(p._sems["VE"], src_ve)
        if h > 0:
            v.wait_ge(p._sems["MM"], p.nb_mm)
        p.vstep(v, v.tensor_mul(qsq[0:64, :], qsq_src_q, qsq_src_q))
        p.vstep(v, v.tensor_mul(qsq[64:128, :], qsq_src_k, qsq_src_k))
    sc_sq = p.n("VE")

    @blk.tensor
    def _(t):
        t.wait_ge(p._sems["VE"], sc_sq)
        if h > 0:
            t.wait_ge(p._sems["VE"], p.nb_ve)
        for tc in range(NTC):
            i1 = t.matmul(qn[:, tc:tc + 1], qsq[0:64, tc * 128:(tc + 1) * 128],
                          p.ones_cb[0:64, :], start=True, stop=True)
            i2 = t.matmul(qn[:, 16 + tc:17 + tc], qsq[64:128, tc * 128:(tc + 1) * 128],
                          p.ones_cb[64:128, :], start=True, stop=True)
        p.inc(i2, "MM")
    p.nb_mm = p.n("MM")

    @blk.vector
    def _(v):
        v.wait_ge(p._sems["MM"], p.nb_mm)
        p.vstep(v, v.reduce_max(p.maxall[:, h:h + 1], qn, axis=AX.X))
    ve_mx = p.n("VE")

    @blk.tensor
    def _(t):
        t.wait_ge(p._sems["VE"], ve_mx)
        if h > 0:
            t.wait_ge(p._sems["VE"], p.nb_ve2)
        p.inc(t.transpose(tr, p.maxall[:, h:h + 1], p.ident_f[:]), "MM")
    mm_tr = p.n("MM")

    @blk.vector
    def _(v):
        v.wait_ge(p._sems["MM"], mm_tr)
        p.vstep(v, v.reduce_max(p.m2sb[:, 0:1], tr, axis=AX.X))
        p.vstep(v, v.tensor_scalar(p.m2sb[:, 1:2], p.m2sb[:, 0:1], -0.125, None, ALU.mult))
        p.vstep(v, v.tensor_copy(p.m2bf[:, 0:1], p.m2sb[:, 1:2]))
    p.nb_ve2 = p.n("VE")
    ve_bf = p.n("VE")

    @blk.tensor
    def _(t):
        t.wait_ge(p._sems["VE"], ve_bf)
        p.inc(t.matmul(ps_misc[:, 32:33], p.ones_rbig[:], p.m2bf[:, 0:1],
                       start=True, stop=True), "MM")
    mm_bc = p.n("MM")

    @blk.vector
    def _(v):
        v.wait_ge(p._sems["MM"], mm_bc)
        p.vstep(v, v.tensor_copy(p.negm8[:, h:h + 1], ps_misc[:, 32:33]))
    p.nb_ve = p.n("VE")


def _attn_head_tt(p, pp, psy, psbc, qap, kap, vhat, h, tt, NT, causal, yun, Ydst,
                  recl_f, recl_b, st, pkey="psyf"):
    blk = p.block
    nh = NT // 512
    sc_list = list(range(4 * tt + 4)) if causal and NT == 512 else list(range(16))
    n_sc = len(sc_list)
    negm8 = p.negm8[:, h:h + 1]
    j0 = st["j"]

    sc_base = p.n("SC")
    ve_base = p.n("VE")
    masked = {}
    nmask = 0
    for k_i, sc in enumerate(sc_list):
        if causal and 4 * tt <= sc <= 4 * tt + 3 and NT == 512:
            nmask += 1
            masked[k_i] = nmask
    exp_cnt = {k_i: sc_base + k_i + 1 for k_i in range(n_sc)}
    mm_s = {}

    @blk.tensor
    def _(t):
        def S(k_i):
            sc = sc_list[k_i]
            j = j0 + k_i
            if j >= 2:
                t.wait_ge(p._sems["SC"], st["exp"][j - 2] if j - 2 < j0
                          else exp_cnt[j - 2 - j0])
            if k_i == 0:
                for key, val in st.pop("first_waits", {}).items():
                    t.wait_ge(p._sems[key], val)
            tile = pp[:, (j % 2) * NT:(j % 2 + 1) * NT]
            for n in range(nh):
                i1 = t.matmul(tile[:, n * 512:(n + 1) * 512],
                              kap[:, sc * 128:(sc + 1) * 128],
                              qap[:, tt * NT + n * 512: tt * NT + n * 512 + 512],
                              start=True, stop=True)
            p.inc(i1, "MM")
            mm_s[k_i] = p.n("MM")

        def A(k_i):
            sc = sc_list[k_i]
            j = j0 + k_i
            t.wait_ge(p._sems["SC"], exp_cnt[k_i])
            if k_i == 0:
                t.wait_ge(p._sems["VE"], st.get(pkey, 0))
            par = j % 2
            for n in range(nh):
                i1 = t.matmul(psy[:, n * 512:(n + 1) * 512],
                              vhat[:, sc * 65:(sc + 1) * 65],
                              st["pt"][par][:, n * 512:(n + 1) * 512],
                              start=(k_i == 0), stop=(k_i == n_sc - 1))
            p.inc(i1, "MM")

        S(0)
        for k_i in range(1, n_sc):
            S(k_i)
            A(k_i - 1)
        A(n_sc - 1)

    if nmask:
        @blk.vector
        def _(v):
            done = 0
            for k_i in sorted(masked):
                o = sc_list[k_i] - 4 * tt
                j = j0 + k_i
                v.wait_ge(p._sems["MM"], mm_s[k_i])
                tile = pp[:, (j % 2) * NT:(j % 2 + 1) * NT]
                p.vinc(v.tensor_add(tile, tile,
                                    p.band_b[:, (3 - o) * 128:(3 - o) * 128 + 512]))
        assert p.n("VE") == ve_base + nmask

    for k_i in range(n_sc):
        j = j0 + k_i
        st["exp"][j] = exp_cnt[k_i]
        st["av"][j] = (mm_s[k_i + 1] + 1) if k_i + 1 < n_sc else (mm_s[n_sc - 1] + 2)

    @blk.scalar
    def _(s):
        for k_i in range(n_sc):
            j = j0 + k_i
            if k_i in masked:
                s.wait_ge(p._sems["VE"], ve_base + masked[k_i])
            else:
                s.wait_ge(p._sems["MM"], mm_s[k_i])
            if j >= 2:
                s.wait_ge(p._sems["MM"], st["av"][j - 2])
            par = j % 2
            p.sinc(s.activation(st["pt"][par], pp[:, par * NT:(par + 1) * NT],
                                AF.Exp, bias=negm8, scale=0.125))
    assert p.n("SC") == sc_base + n_sc
    st["j"] = j0 + n_sc
    mm_avl = p.n("MM")

    @blk.vector
    def _(v):
        v.wait_ge(p._sems["MM"], mm_avl)
        p.vstep(v, v.reciprocal(recl_f[:, 0:NT], psy[64:65, 0:NT]))
        p.vstep(v, v.tensor_copy(recl_b[:, 0:NT], recl_f[:, 0:NT]))
        if st.get("yun_ve"):
            v.wait_ge(p._sems["VE"], st["yun_ve"])
        p.vstep(v, v.tensor_copy(yun[0:64, 0:NT], psy[0:64, 0:NT]))
    ve_rc = p.n("VE")
    sc_yun = p.n("VE")

    for n in range(nh):
        @blk.tensor
        def _(t):
            t.wait_ge(p._sems["VE"], ve_rc)
            if st.get("bc_ve"):
                t.wait_ge(p._sems["VE"], st["bc_ve"])
            p.inc(t.matmul(psbc[:], p.ones_rbig[0:1, 0:64],
                           recl_b[:, n * 512:(n + 1) * 512], start=True, stop=True), "MM")
        mm_bc = p.n("MM")

        @blk.vector
        def _(v):
            v.wait_ge(p._sems["MM"], mm_bc)
            p.vstep(v, v.tensor_mul(Ydst[:, n * 512:(n + 1) * 512],
                                    yun[0:64, n * 512:(n + 1) * 512], psbc[0:64, :]))
        st["bc_ve"] = p.n("VE")
    st["yun_ve"] = p.n("VE")
    st[pkey] = p.n("VE")


def emit_attn_l0(p, d_wcd, d_wqu, d_wkv, d_wcp):
    blk = p.block
    NT = 512
    wcd = abf(p, 0, 4096)
    wqu = abf(p, 8192, 4096)
    wkv = abf(p, 16384, 512)
    latent = abf(p, 17408, 8192)
    kv = abf(p, 33792, 2048)
    qpair = [abf(p, 37888, 2048), abf(p, 41984, 2048)]
    qsq = abf(p, 46080, 2048)
    Y = abf(p, 50176, 16384)
    vhat = abf(p, 0, 16 * 65)
    ptile = [abf(p, 2080, 512), abf(p, 3104, 512)]
    yun = abf(p, 4128, 512)
    recl_f = af32(p, 5152, 512, parts=1)
    recl_b = abf(p, 16384, 512, parts=1)
    wcp = abf(p, 17408, 8192)
    ctx = ExitStack()
    pp = ctx.enter_context(p.nc.psum_tensor("pp_a0", [128, 1024], F32))
    psy = ctx.enter_context(p.nc.psum_tensor("psy_a0", [65, 2 * NT], F32))
    psbc = ctx.enter_context(p.nc.psum_tensor("psbc_a0", [64, 512], F32))
    ps_misc = ctx.enter_context(p.nc.psum_tensor("psm_a0", [128, 256], F32))

    @blk.gpsimd
    def _(g):
        _gload(p, g, [(_r3(wcd, 8), d_wcd), (_r3(wqu, 4), d_wqu),
                      (_r3(wkv, 4), d_wkv)],
               gate_mm=p.n("MM"), gate_all=True)
    ld_w = p.n("LD")

    evacs = {}
    jq = [0]

    def proj_group(nk, lhs_fn, dst, extra_waits=(), width=512, rows=128):
        j = jq[0]

        @blk.tensor
        def _(t):
            t.wait_ge(p._sems["LD"], ld_w)
            if j == 0:
                t.wait_ge(p._sems["VE"], p.xn_ve)
            if j >= 2:
                t.wait_ge(p._sems["VE"], evacs[j - 2])
            for sk, sv in extra_waits:
                t.wait_ge(p._sems[sk], sv)
            for c in range(nk):
                i1 = t.matmul(pp[0:rows, (j % 2) * 512:(j % 2) * 512 + width],
                              *lhs_fn(c), start=(c == 0), stop=(c == nk - 1))
            p.inc(i1, "MM")
        mm_g = p.n("MM")

        @blk.vector
        def _(v):
            v.wait_ge(p._sems["MM"], mm_g)
            p.vinc(v.tensor_copy(dst, pp[0:rows, (j % 2) * 512:(j % 2) * 512 + width]))
        evacs[j] = p.n("VE")
        jq[0] += 1
        return p.n("VE")

    for lc in range(4):
        for n in range(4):
            proj_group(8, (lambda lc, n: lambda dc: (
                wcd[:, dc * 512 + lc * 128: dc * 512 + lc * 128 + 128],
                p.xn_fm[:, dc * 2048 + n * 512: dc * 2048 + n * 512 + 512]))(lc, n),
                latent[:, lc * 2048 + n * 512: lc * 2048 + n * 512 + 512])
    sc_lat = p.n("VE")
    @blk.vector
    def _(v):
        p.vstep(v, v.memset(vhat[:], 1.0))
    ve_vh0 = p.n("VE")
    for n in range(4):
        proj_group(4, (lambda n: lambda lc: (
            wkv[:, lc * 128: lc * 128 + 64],
            latent[:, lc * 2048 + n * 512: lc * 2048 + n * 512 + 512]))(n),
            kv[0:64, n * 512:(n + 1) * 512], rows=64,
            extra_waits=[("VE", sc_lat)] if n == 0 else ())
    sc_k = p.n("VE")

    @blk.vector
    def _(v):
        v.wait_ge(p._sems["VE"], sc_k)
        p.vstep(v, v.tensor_copy(kv[64:128, :], kv[0:64, :]))
    ve_kcopy = p.n("VE")

    for tc in range(NTC):
        proj_group(4, (lambda tc: lambda lc: (
            latent[:, lc * 2048 + tc * 128: lc * 2048 + tc * 128 + 128],
            wkv[:, lc * 128 + 64: lc * 128 + 128]))(tc),
            vhat[:, tc * 65: tc * 65 + 64], width=64,
            extra_waits=[("VE", ve_vh0)] if tc == 0 else ())
    sc_kv = p.n("VE")
    ve_vhat = p.n("VE")

    st = {"j": 0, "exp": {}, "av": {}, "pt": ptile}
    pair_ve = {}
    last_q_mm = 0
    for g in range(8):
        qslot = qpair[g % 2]
        extra = []
        if g >= 2:
            extra.append(("VE", pair_ve[g - 2]))
        if g >= 1:
            extra.append(("SC", p.n("SC")))
            extra.append(("VE", p.n("VE")))
        for n in range(4):
            sc_q = proj_group(4, (lambda g, n: lambda lc: (
                wqu[:, lc * 1024 + g * 128: lc * 1024 + g * 128 + 128],
                latent[:, lc * 2048 + n * 512: lc * 2048 + n * 512 + 512]))(g, n),
                qslot[:, n * 512:(n + 1) * 512],
                extra_waits=extra if n == 0 else ())
        last_q_mm = p.n("MM")
        if g == 7:
            @blk.gpsimd
            def _(g_):
                _gload(p, g_, [(_r3(wcp, 8), d_wcp)],
                       gate_mm=last_q_mm, gate_all=False)
            p.ld_wcp = p.n("LD")
        for hh in range(2):
            h = 2 * g + hh
            kap = kv[64 * hh: 64 * hh + 64, :]
            _norm_bound(p, ps_misc, qslot[64 * hh: 64 * hh + 64, :], kv[0:64, :],
                        qsq, h, p.n("MM"))
            for tt in range(4):
                st["first_waits"] = {"VE": max(ve_vhat, p.nb_ve, ve_kcopy, sc_q)}
                _attn_head_tt(p, pp, psy[:, (tt % 2) * NT:(tt % 2 + 1) * NT], psbc,
                              qslot[64 * hh: 64 * hh + 64, :],
                              kap, vhat, h, tt, NT, True, yun,
                              Y[64 * hh: 64 * hh + 64,
                                g * 2048 + tt * NT: g * 2048 + (tt + 1) * NT],
                              recl_f, recl_b, st, pkey="psyf%d" % (tt % 2))
        pair_ve[g] = p.n("VE")
    ve_heads = p.n("VE")

    pove = {}
    npo = 0
    for tc in range(NTC):
        for fh in range(2):
            @blk.tensor
            def _(t):
                t.wait_ge(p._sems["LD"], p.ld_wcp)
                if npo == 0:
                    t.wait_ge(p._sems["VE"], ve_heads)
                    t.wait_ge(p._sems["SC"], p.n("SC"))
                if npo >= 2:
                    t.wait_ge(p._sems["VE"], pove[npo - 2])
                for dc in range(8):
                    i1 = t.matmul(pp[:, (npo % 2) * 512:(npo % 2) * 512 + 512],
                                  Y[:, dc * 2048 + tc * 128: dc * 2048 + tc * 128 + 128],
                                  wcp[:, dc * 1024 + fh * 512: dc * 1024 + fh * 512 + 512],
                                  start=(dc == 0), stop=(dc == 7))
                p.inc(i1, "MM")
            mm_pr = p.n("MM")

            @blk.vector
            def _(v):
                v.wait_ge(p._sems["MM"], mm_pr)
                xs = p.x_tm[:, tc * 1024 + fh * 512: tc * 1024 + fh * 512 + 512]
                pp_s = pp[:, (npo % 2) * 512:(npo % 2) * 512 + 512]
                p.xadd(v, tc * 2 + fh, lambda: v.tensor_add(xs, xs, pp_s))
            pove[npo] = p.n("VE")
            npo += 1
    p.attn0_mm = p.n("MM")
    ctx.close()


def emit_attn_l1(p, d_qkvw, d_qkvb, d_outw, d_outb):
    blk = p.block
    NT = 1024
    qkvw = [abf(p, 0, 3072), abf(p, 6144, 3072)]
    qt = abf(p, 12288, 2048)
    kt = abf(p, 16384, 2048)
    vhat2 = [abf(p, 24576, 16 * 65), abf(p, 26656, 16 * 65)]
    ptile = [abf(p, 28736, 1024), abf(p, 30784, 1024)]
    qsq = abf(p, 32832, 2048)
    yun = abf(p, 36928, 1024)
    recl_f = af32(p, 38976, 1024, parts=1)
    recl_b = abf(p, 43072, 1024, parts=1)
    qkvb_row = abf(p, 45120, 3072, parts=1)
    Y = abf(p, 51264, 16384)
    outb_row = abf(p, 84032, 1024, parts=1)
    outw = abf(p, 0, 8192)
    ctx = ExitStack()
    pp = ctx.enter_context(p.nc.psum_tensor("pp_a1", [128, 2048], F32))
    psy = ctx.enter_context(p.nc.psum_tensor("psy_a1", [65, NT], F32))
    misc2 = ctx.enter_context(p.nc.psum_tensor("psbc_a1", [128, 512], F32))
    psbc = misc2[0:64, 0:512]
    ps_misc = ctx.enter_context(p.nc.psum_tensor("psm_a1", [128, 256], F32))

    @blk.gpsimd
    def _(g):
        _gload(p, g, [(qkvb_row, d_qkvb), (outb_row, d_outb)],
               gate_mm=p.n("MM"), gate_all=True)
    ld_b = p.n("LD")

    def qkvsrc(gidx):
        def f(slot):
            return (_r3(slot, 8), d_qkvw[gidx])
        return f

    s_qkv = SegStream(p, "qkvw", qkvw, [qkvsrc(gi) for gi in range(8)])
    s_qkv.prime(gate_mm=p.n("MM"))

    st = {"j": 0, "exp": {}, "av": {}, "pt": ptile}
    jq = 0
    evq = {}
    for g in range(8):
        wbuf, wld = s_qkv.use(g)
        for c in range(2):
            for n in range(2):
                @blk.tensor
                def _(t):
                    t.wait_ge(p._sems["LD"], wld)
                    if g == 0 and c == 0 and n == 0:
                        t.wait_ge(p._sems["VE"], p.xn_ve)
                        t.wait_ge(p._sems["LD"], ld_b)
                    if jq >= 2:
                        t.wait_ge(p._sems["VE"], evq[jq - 2])
                    if st["j"] >= 1:
                        t.wait_ge(p._sems["SC"], st["exp"][st["j"] - 1])
                    for half in range(2):
                        for dc in range(8):
                            t.matmul(
                                pp[:, (jq % 2) * 1024 + half * 512:(jq % 2) * 1024 + half * 512 + 512],
                                wbuf[:, dc * 384 + c * 128: dc * 384 + c * 128 + 128],
                                p.xn_fm[:, dc * 2048 + n * 1024 + half * 512: dc * 2048 + n * 1024 + half * 512 + 512],
                                start=(dc == 0), stop=False)
                        i1 = t.matmul(
                            pp[:, (jq % 2) * 1024 + half * 512:(jq % 2) * 1024 + half * 512 + 512],
                            qkvb_row[:, g * 384 + c * 128: g * 384 + c * 128 + 128],
                            p.ones_rb[:], start=False, stop=True)
                    p.inc(i1, "MM")
                mm_q = p.n("MM")

                @blk.vector
                def _(v):
                    v.wait_ge(p._sems["MM"], mm_q)
                    if g > 0 and c == 0 and n == 0:
                        v.wait_ge(p._sems["MM"], p.pair_done_mm)
                    src = pp[:, (jq % 2) * 1024:(jq % 2) * 1024 + 1024]
                    ts_ = slice(n * 1024, (n + 1) * 1024)
                    if c == 0:
                        p.vinc(v.tensor_copy(qt[0:64, ts_], src[0:64, :]))
                        p.vinc(v.tensor_copy(kt[0:64, ts_], src[64:128, :]))
                    else:
                        p.vinc(v.tensor_copy(qt[64:128, ts_], src[0:64, :]))
                        p.vinc(v.tensor_copy(kt[64:128, ts_], src[64:128, :]))
                evq[jq] = p.n("VE")
                jq += 1
        for hh in range(2):
            @blk.vector
            def _(v):
                if g > 0:
                    v.wait_ge(p._sems["MM"], p.pair_done_mm)
                p.vstep(v, v.memset(vhat2[hh][:], 1.0))
        ve_vh0 = p.n("VE")
        for half in range(2):
            @blk.tensor
            def _(t):
                if jq >= 2:
                    t.wait_ge(p._sems["VE"], evq[jq - 2])
                if st["j"] >= 1:
                    t.wait_ge(p._sems["SC"], st["exp"][st["j"] - 1])
                for tci in range(8):
                    tc = half * 8 + tci
                    base = (jq % 2) * 1024 + tci * 128
                    for dc in range(8):
                        t.matmul(pp[:, base: base + 128],
                                 p.xn_fm[:, dc * 2048 + tc * 128: dc * 2048 + tc * 128 + 128],
                                 wbuf[:, dc * 384 + 256: dc * 384 + 384],
                                 start=(dc == 0), stop=False)
                    i1 = t.matmul(pp[:, base: base + 128],
                                  p.ones_rb[0:1, 0:128],
                                  qkvb_row[:, g * 384 + 256: g * 384 + 384],
                                  start=False, stop=True)
                p.inc(i1, "MM")
            mm_v = p.n("MM")

            @blk.vector
            def _(v):
                v.wait_ge(p._sems["MM"], mm_v)
                v.wait_ge(p._sems["VE"], ve_vh0)
                for tci in range(8):
                    tc = half * 8 + tci
                    base = (jq % 2) * 1024 + tci * 128
                    p.vinc(v.tensor_copy(vhat2[0][:, tc * 65: tc * 65 + 64],
                                         pp[:, base: base + 64]))
                    p.vinc(v.tensor_copy(vhat2[1][:, tc * 65: tc * 65 + 64],
                                         pp[:, base + 64: base + 128]))
            evq[jq] = p.n("VE")
            jq += 1
        s_qkv.done(g, p.n("MM"))
        for fn in getattr(p, "idx_scatters", [])[2 * g: 2 * g + 2]:
            fn()
        sc_pair = p.n("VE")
        ve_vhat = p.n("VE")

        for hh in range(2):
            h = 2 * g + hh
            qap = qt[64 * hh: 64 * hh + 64, :]
            kap = kt[64 * hh: 64 * hh + 64, :]
            _norm_bound(p, ps_misc, qap, kap, qsq, h, p.n("MM"))
            for tt in range(2):
                st["first_waits"] = {"VE": max(ve_vhat, p.nb_ve, sc_pair)}
                _attn_head_tt(p, pp, psy, psbc, qap, kap,
                              vhat2[hh], h, tt, NT, False, yun,
                              Y[64 * hh: 64 * hh + 64,
                                g * 2048 + tt * NT: g * 2048 + (tt + 1) * NT],
                              recl_f, recl_b, st)
        p.pair_done_mm = p.n("MM")
    ve_heads = p.n("VE")

    @blk.gpsimd
    def _(g_):
        _gload(p, g_, [(_r3(outw, 8), d_outw)],
               gate_mm=p.n("MM"), gate_all=True)
    ld_outw = p.n("LD")

    pove = {}
    npo = 0
    for tc in range(NTC):
        for fh in range(2):
            @blk.tensor
            def _(t):
                t.wait_ge(p._sems["LD"], ld_outw)
                if npo == 0:
                    t.wait_ge(p._sems["VE"], ve_heads)
                    t.wait_ge(p._sems["SC"], p.n("SC"))
                if npo >= 2:
                    t.wait_ge(p._sems["VE"], pove[npo - 2])
                for dc in range(8):
                    t.matmul(pp[:, (npo % 2) * 512:(npo % 2) * 512 + 512],
                             Y[:, dc * 2048 + tc * 128: dc * 2048 + tc * 128 + 128],
                             outw[:, dc * 1024 + fh * 512: dc * 1024 + fh * 512 + 512],
                             start=(dc == 0), stop=False)
                i1 = t.matmul(pp[:, (npo % 2) * 512:(npo % 2) * 512 + 512],
                              p.ones_rb[0:1, 0:128],
                              outb_row[:, fh * 512:(fh + 1) * 512],
                              start=False, stop=True)
                p.inc(i1, "MM")
            mm_pr = p.n("MM")

            @blk.vector
            def _(v):
                v.wait_ge(p._sems["MM"], mm_pr)
                xs = p.x_tm[:, tc * 1024 + fh * 512: tc * 1024 + fh * 512 + 512]
                pp_s = pp[:, (npo % 2) * 512:(npo % 2) * 512 + 512]
                p.xadd(v, tc * 2 + fh, lambda: v.tensor_add(xs, xs, pp_s))
            pove[npo] = p.n("VE")
            npo += 1
    p.attn1_mm = p.n("MM")
    ctx.close()



def _prep_host(I):
    H = {}
    H["x_sh"] = [np.ascontiguousarray(I["L0_attn_x"][c]) for c in range(8)]
    H["gwT0"] = np.ascontiguousarray(I["L0_ffn_gate_w"].T)
    H["b1fm"] = np.ascontiguousarray(
        I["L1_ffn_b1"].reshape(16, 16, 128).transpose(2, 0, 1).reshape(128, 256))
    H["qkvb"] = np.ascontiguousarray(I["L1_attn_qkv_b"].reshape(1, 3072))
    H["outb"] = np.ascontiguousarray(I["L1_attn_out_b"].reshape(1, 1024))
    H["rb"] = np.ascontiguousarray(I["L1_ffn_router_b"].reshape(1, 16))
    H["w1s"] = np.ascontiguousarray(
        I["L1_ffn_w1"].reshape(16, 8, 128, 4, 512).transpose(0, 2, 3, 1, 4))
    H["w2s"] = np.ascontiguousarray(
        I["L1_ffn_w2"].reshape(16, 16, 128, 2, 512).transpose(0, 2, 3, 1, 4))
    gus = I["L0_ffn_gate_up_proj"].reshape(8, 8, 128, 4, 512)
    H["gus"] = np.ascontiguousarray(gus[:, :, :, [0, 2, 1, 3], :].transpose(0, 2, 3, 1, 4))
    H["dns"] = np.ascontiguousarray(
        I["L0_ffn_down_proj"].reshape(8, 8, 128, 2, 512).transpose(0, 2, 3, 1, 4))
    q6 = I["L1_attn_qkv_w"].reshape(8, 128, 8, 6, 64)
    H["qkvs"] = np.ascontiguousarray(
        q6[:, :, :, [0, 1, 3, 4, 2, 5], :].transpose(2, 1, 0, 3, 4)
        .reshape(8, 128, 8, 384))
    b6 = I["L1_attn_qkv_b"].reshape(8, 6, 64)
    H["qkvb2"] = np.ascontiguousarray(b6[:, [0, 1, 3, 4, 2, 5], :].reshape(1, 3072))
    H["wcds"] = np.ascontiguousarray(I["L0_attn_Wc_down"].reshape(8, 128, 512).transpose(1, 0, 2))
    H["wqus"] = np.ascontiguousarray(I["L0_attn_Wq_up"].reshape(4, 128, 1024).transpose(1, 0, 2))
    H["wkvs"] = np.ascontiguousarray(I["L0_attn_Wkv_up"].reshape(4, 128, 128).transpose(1, 0, 2))
    H["wcps"] = np.ascontiguousarray(I["L0_attn_Wc_proj"].reshape(8, 128, 1024).transpose(1, 0, 2))
    H["outws"] = np.ascontiguousarray(I["L1_attn_out_w"].reshape(8, 128, 1024).transpose(1, 0, 2))
    return H


def _emit_l0(p, I, H, d_x):
    cd = host_consts()
    emit_consts(p, cd)
    emit_load_x(p, d_x)
    alloc_arena(p)
    alloc_ln(p)
    d_gwT = p.din("gwT0", H["gwT0"])
    d_wcd = p.din("wcd", H["wcds"])
    d_wqu = p.din("wqu", H["wqus"])
    d_wkv = p.din("wkv", H["wkvs"])
    d_wcp = p.din("wcp", H["wcps"])
    d_gup = p.din("gup", H["gus"])
    d_dp = p.din("dp", H["dns"])
    with p.nc.named_scope("LN0"):
        emit_ln_router(p, "L0", 8, 2, d_gwT, None)
    emit_barrier(p)
    with p.nc.named_scope("ATT0"):
        emit_attn_l0(p, d_wcd, d_wqu, d_wkv, d_wcp)
    emit_barrier(p)
    with p.nc.named_scope("FFN0"):
        emit_ffn_l0(p, d_gup, d_dp)


def _emit_l1(p, I, H):
    d_rw = p.din("rw", I["L1_ffn_router_w"])
    d_rb = p.din("rb", H["rb"])
    d_qkvw = p.din("qkvw", H["qkvs"])
    d_qkvb = p.din("qkvb", H["qkvb2"])
    d_outw = p.din("outw", H["outws"])
    d_outb = p.din("outb", H["outb"])
    d_w1 = p.din("w1", H["w1s"])
    d_b1fm = p.din("b1fm", H["b1fm"])
    d_w2 = p.din("w2", H["w2s"])
    d_b2 = p.din("b2", I["L1_ffn_b2"])
    with p.nc.named_scope("LN1"):
        emit_ln_router(p, "L1", 16, 4, d_rw, d_rb, sparse=True)
    with p.nc.named_scope("IDX1"):
        emit_idx_build(p)
    emit_barrier(p)
    with p.nc.named_scope("ATT1"):
        emit_attn_l1(p, d_qkvw, d_qkvb, d_outw, d_outb)
    emit_barrier(p)
    with p.nc.named_scope("FFN1"):
        emit_ffn_l1_sparse(p, d_w1, d_b1fm, d_w2, d_b2)


def build_layer0(I, H):
    p = Prog()
    d_x = p.din("x", H["x_sh"], sharded=True)
    d_x1 = p.dout("x1", [T, D])
    p.start()
    _emit_l0(p, I, H, d_x)
    emit_store_x(p, d_x1, p.x_tm)
    p.done()
    return p


def build_layer1(I, H, x1_sh):
    p = Prog()
    d_x = p.din("x", x1_sh, sharded=True)
    d_y = p.dout("y", [T, D])
    p.start()
    cd = host_consts()
    emit_consts(p, cd)
    emit_load_x(p, d_x)
    alloc_arena(p)
    alloc_ln(p)
    _emit_l1(p, I, H)
    emit_store_acc(p, d_y)
    p.done()
    return p


def build_fused(I, H):
    p = Prog()
    d_x = p.din("x", H["x_sh"], sharded=True)
    d_y = p.dout("y", [T, D])
    p.start()
    _emit_l0(p, I, H, d_x)
    emit_barrier(p)
    p.x_ready = [("VE", p.n("VE"))]
    _emit_l1(p, I, H)
    emit_store_acc(p, d_y)
    p.done()
    return p


FUSED = True


def kernel(**inputs):
    I = {k: np.asarray(v) for k, v in inputs.items()}
    H = _prep_host(I)
    if FUSED:
        p = build_fused(I, H)
        res = p.run()
        return np.stack([res.results[c]["y"] for c in range(8)], axis=0)
    p0 = build_layer0(I, H)
    r0 = p0.run()
    x1_sh = [np.ascontiguousarray(r0.results[c]["x1"]) for c in range(8)]
    p1 = build_layer1(I, H, x1_sh)
    r1 = p1.run()
    return np.stack([r1.results[c]["y"] for c in range(8)], axis=0)



I32 = mybir.dt.int32
CAP = 1152
NCS = CAP // 128
BIGV = 1.0e5


def _xr_f(p, off, n, parts=128):
    return p.x_tm[0:parts, off:off + n]


def _xr_b(p, off, n, parts=128):
    return p.x_tm[:].bitcast(BF16)[0:parts, 2 * off: 2 * off + n]


def emit_ffn_l1_sparse(p, d_w1, d_b1fm, d_w2, d_b2):
    blk = p.block
    E = 16
    b1_sb = af32(p, 88064, 16 * E)
    b2_sb = abf(p, 86016, 1024, parts=16)
    w1slots = [abf(p, 0, 4096), abf(p, 8192, 4096)]
    w2slots = [abf(p, 16384, 8192), abf(p, 32768, 8192)]
    hmid = abf(p, 49152, 16 * CAP)
    xg_tm = [_xr_b(p, 0, NCS * 1024), _xr_b(p, 0, NCS * 1024)]
    xg_fm2 = [_xr_b(p, 4608, 8 * CAP), _xr_b(p, 9216, 8 * CAP)]
    stage = [_xr_f(p, 13824, 512), _xr_f(p, 14336, 512)]
    idxf = _xr_f(p, 15872, 2 * NCS)
    cg = _xr_f(p, 15890, 2 * NCS)
    idxi = p.x_tm[:].bitcast(I32)[:, 15958:15976]
    cgxf = _xr_f(p, 16008, 1)
    cgxi = p.x_tm[:].bitcast(I32)[:, 16010:16011]
    p.ACC = p.nc.dram_tensor("ACC_l1", [T, D], F32, kind="Internal").ap()
    IDX = p.IDX

    ctx = ExitStack()
    ph = ctx.enter_context(p.nc.psum_tensor("ph_l1", [128, 2048], F32))
    po = ctx.enter_context(p.nc.psum_tensor("po_l1", [128, 1024], F32))
    pcb = ctx.enter_context(p.nc.psum_tensor("pcb_l1", [128, 512], F32))
    pos_ps = pcb[:, 0:16]
    off_ps = pcb[0:1, 128:144]

    @blk.sync
    def _(y):
        p.wait(y, "MM")
        p.inc(y.dma_start(out=b1_sb[:], in_=d_b1fm[:]), "LI", 16)
        y.wait_ge(p._sems["LI"], p.c["LI"])
    li_b1 = p.n("LI")

    @blk.gpsimd
    def _(g):
        _gload(p, g, [(b2_sb[:], d_b2[:])], gate_mm=p.n("MM"), gate_all=True)
    ld_b2 = p.n("LD")

    seq = list(range(E))
    s_w1 = SegStream(p, "w1", w1slots,
                     [(lambda e, q: lambda s: (_r3(s, 8), d_w1[e, :, q]))(e, q)
                      for e in seq for q in range(4)])
    s_w2 = SegStream(p, "w2", w2slots,
                     [(lambda e, h: lambda s: (_r3(s, 16), d_w2[e, :, h]))(e, h)
                      for e in seq for h in range(2)])
    s_w1.prime(gate_mm=p.n("MM"))
    s_w2.prime(gate_mm=p.n("MM"))

    cb_prev = 0
    for tc in range(NTC):
        for h in range(2):
            @blk.tensor
            def _(t):
                t.wait_ge(p._sems["VE"], p.xn_ve)
                t.wait_ge(p._sems["LD"], ld_b2)
                if cb_prev:
                    t.wait_ge(p._sems["VE"], cb_prev)
                p.inc(t.matmul(pcb[:], p.CT_sb[0:16, tc * 128:(tc + 1) * 128],
                               b2_sb[0:16, h * 512:(h + 1) * 512],
                               start=True, stop=True), "MM")
            mm_cb = p.n("MM")

            @blk.vector
            def _(v):
                v.wait_ge(p._sems["MM"], mm_cb)
                xs = p.x_tm[:, tc * 1024 + h * 512: tc * 1024 + (h + 1) * 512]
                p.xadd(v, tc * 2 + h, lambda: v.tensor_add(xs, xs, pcb[:]))
            cb_prev = p.n("VE")
    ve_cb2 = p.n("VE")

    @blk.gpsimd
    def _(g):
        g.wait_ge(p._sems["VE"], ve_cb2)
        for k in ("SC", "MM"):
            if p.c[k]:
                g.wait_ge(p._sems[k], p.c[k])
        dst = p.ACC.rearrange("(j p) d -> p j d", p=128)
        p.inc(g.dma_start(out=dst, in_=p.x_tm[:].rearrange("p (j d) -> p j d", j=NTC)),
              "LD", 16)
        g.wait_ge(p._sems["LD"], p.c["LD"])
    ld_init = p.n("LD")


    gath_ld = {}
    idxi_ve = {}
    tr_li = {}
    cgx_war = [0]
    w1_done_mm = {}
    w2_done_mm = {}

    def emit_gather(e):
        par = e % 2

        @blk.gpsimd
        def _(g):
            if e >= 1:
                g.wait_ge(p._sems["LI"], tr_li[e - 1])
            for cs in range(NCS):
                p.inc(g.dma_start(out=idxf[:, par * NCS + cs: par * NCS + cs + 1],
                                  in_=IDX[e * T + cs * 128: e * T + (cs + 1) * 128, 0:1]),
                      "LD", 16)
            g.wait_ge(p._sems["LD"], p.c["LD"])
        ld_i = p.n("LD")

        @blk.vector
        def _(v):
            v.wait_ge(p._sems["LD"], ld_i)
            p.vstep(v, v.tensor_copy(idxi[:, par * NCS: par * NCS + NCS],
                                     idxf[:, par * NCS: par * NCS + NCS]))
        idxi_ve[e] = p.n("VE")

        cflat = p.C_hbm
        for cs in range(NCS):
            @blk.vector
            def _(v):
                v.wait_ge(p._sems["LD"], ld_i)
                if cs == 0:
                    v.wait_ge(p._sems["VE"], p.c["VE"] - 0)
                if cgx_war[0]:
                    v.wait_ge(p._sems["LD"], cgx_war[0])
                p.vstep(v, v.tensor_scalar(cgxf, idxf[:, par * NCS + cs: par * NCS + cs + 1],
                                           16.0, float(e), ALU.mult, ALU.add))
                p.vstep(v, v.tensor_copy(cgxi, cgxf))
            ve_cgx = p.n("VE")

            @blk.gpsimd
            def _(g):
                g.wait_ge(p._sems["VE"], ve_cgx)
                ic = idxi[:, par * NCS + cs: par * NCS + cs + 1]
                p.inc(g.indirect_dma_start(
                    xg_tm[par][:, cs * 1024:(cs + 1) * 1024], None,
                    p.xn_hbm, IndirectOffsetOnAxis(ic, 0),
                    bounds_check=p.bc_t, oob_is_err=False), "LD", 16)
                p.inc(g.indirect_dma_start(
                    cg[:, par * NCS + cs: par * NCS + cs + 1], None,
                    cflat, IndirectOffsetOnAxis(cgxi, 0),
                    bounds_check=p.bc_et, oob_is_err=False), "LD", 16)
                g.wait_ge(p._sems["LD"], p.c["LD"])
            cgx_war[0] = p.n("LD")
        gath_ld[e] = p.n("LD")

    emit_gather(0)
    jg = 0
    npo = 0
    gel = {}
    pove = {}
    stage_war = {0: ld_init, 1: ld_init}
    for e in range(E):
        par = e % 2
        @blk.sync
        def _(y):
            y.wait_ge(p._sems["LD"], gath_ld[e])
            if e == 0:
                y.wait_ge(p._sems["LD"], ld_init)
            if e >= 2:
                y.wait_ge(p._sems["MM"], w1_done_mm[e - 2])
            for cs in range(NCS):
                for b in range(8):
                    i1 = y.dma_start(
                        out=xg_fm2[par][:, b * CAP + cs * 128: b * CAP + cs * 128 + 128],
                        in_=xg_tm[par][:, cs * 1024 + b * 128: cs * 1024 + b * 128 + 128],
                        transpose=True)
                    p.inc(i1, "LI", 16)
            y.wait_ge(p._sems["LI"], p.c["LI"])
        tr_li[e] = p.n("LI")
        if e + 1 < E:
            emit_gather(e + 1)

        jg0 = jg
        sc0 = p.n("SC")
        mm_as = {}
        subt = ((0, 512), (512, 512), (1024, 128))
        sub_of = {}
        for ic in range(16):
            q = ic // 4
            w1buf, w1ld = s_w1.use(e * 4 + q)

            @blk.tensor
            def _(t):
                t.wait_ge(p._sems["LD"], w1ld)
                if ic == 0:
                    t.wait_ge(p._sems["LI"], tr_li[e])
                for si, (n0, nw) in enumerate(subt):
                    j = jg0 + ic * 3 + si
                    if j >= 4:
                        t.wait_ge(p._sems["SC"], gel[j - 4])
                    for dc in range(8):
                        i1 = t.matmul(
                            ph[:, (j % 4) * 512:(j % 4) * 512 + nw],
                            w1buf[:, dc * 512 + (ic % 4) * 128: dc * 512 + (ic % 4) * 128 + 128],
                            xg_fm2[par][:, dc * CAP + n0: dc * CAP + n0 + nw],
                            start=(dc == 0), stop=(dc == 7))
                    p.inc(i1, "MM")
                    mm_as[(ic, si)] = p.n("MM")
                    sub_of[j] = (ic, si, n0, nw)
                    gel[j] = sc0 + ic * 3 + si + 1
            if ic % 4 == 3:
                s_w1.done(e * 4 + q, p.n("MM"))
        jg = jg0 + 48
        w1_done_mm[e] = p.n("MM")

        @blk.scalar
        def _(s):
            for j in range(jg0, jg):
                ic, si, n0, nw = sub_of[j]
                s.wait_ge(p._sems["MM"], mm_as[(ic, si)])
                if j == jg0:
                    s.wait_ge(p._sems["LI"], li_b1)
                    if e >= 1:
                        s.wait_ge(p._sems["MM"], w2_done_mm[e - 1])
                p.sinc(s.activation(hmid[:, ic * CAP + n0: ic * CAP + n0 + nw],
                                    ph[:, (j % 4) * 512:(j % 4) * 512 + nw],
                                    AF.Gelu, bias=b1_sb[:, e * 16 + ic: e * 16 + ic + 1]))
        assert p.n("SC") == sc0 + 48
        sc_a_done = p.n("SC")

        for h in range(2):
            w2buf, w2ld = s_w2.use(e * 2 + h)
            for cs in range(NCS):
                @blk.tensor
                def _(t):
                    t.wait_ge(p._sems["LD"], w2ld)
                    if cs == 0 and h == 0:
                        t.wait_ge(p._sems["SC"], sc_a_done)
                    if npo >= 2:
                        t.wait_ge(p._sems["VE"], pove[npo - 2])
                    for ic in range(16):
                        i1 = t.matmul(po[:, (npo % 2) * 512:(npo % 2) * 512 + 512],
                                      hmid[:, ic * CAP + cs * 128: ic * CAP + cs * 128 + 128],
                                      w2buf[:, ic * 512:(ic + 1) * 512],
                                      start=(ic == 0), stop=(ic == 15))
                    p.inc(i1, "MM")
                mm_b = p.n("MM")

                @blk.vector
                def _(v):
                    v.wait_ge(p._sems["MM"], mm_b)
                    if stage_war[npo % 2]:
                        v.wait_ge(p._sems["LD"], stage_war[npo % 2])
                    p.vstep(v, v.tensor_scalar(
                        stage[npo % 2][:],
                        po[:, (npo % 2) * 512:(npo % 2) * 512 + 512],
                        cg[:, par * NCS + cs: par * NCS + cs + 1], None, ALU.mult))
                pove[npo] = p.n("VE")
                ve_st = p.n("VE")

                @blk.gpsimd
                def _(g):
                    g.wait_ge(p._sems["VE"], ve_st)
                    p.inc(g.indirect_dma_start(
                        p.ACC, IndirectOffsetOnAxis(idxi[:, par * NCS + cs: par * NCS + cs + 1], 0),
                        stage[npo % 2][:], None, compute_op=ALU.add,
                        element_offset=h * 512,
                        bounds_check=p.bc_t, oob_is_err=False), "LD", 16)
                    g.wait_ge(p._sems["LD"], p.c["LD"])
                stage_war[npo % 2] = p.n("LD")
                npo += 1
            s_w2.done(e * 2 + h, p.n("MM"))
        w2_done_mm[e] = p.n("MM")
    p.ffn1_mm = p.n("MM")
    ctx.close()


def emit_store_acc(p, d_y):
    blk = p.block

    @blk.sync
    def _(y):
        for k in ("VE", "SC", "MM", "LD", "LI", "LC", "LX"):
            if p.c[k]:
                p.wait(y, k)
        p.inc(y.dma_start(out=d_y, in_=p.ACC), "OUT", 16)
        p.wait(y, "OUT")


def emit_idx_build(p):
    blk = p.block
    E = 16
    big = p.arena[:].bitcast(F32)[:, 0:256]
    offrow = p.arena[:].bitcast(F32)[0:1, 256:512]
    atmp = p.arena[:].bitcast(F32)[:, 256:272]
    posm = p.arena[:].bitcast(F32)[:, 272:288]
    posi_all = p.arena[:].bitcast(I32)[:, 22272:22528]
    p.IDX = p.nc.dram_tensor("IDX_l1", [E * T, 1], F32, kind="Internal").ap()
    ctx = ExitStack()
    pall = ctx.enter_context(p.nc.psum_tensor("p_idx", [128, 256], F32))
    pcnt = ctx.enter_context(p.nc.psum_tensor("p_cnt", [1, 256], F32))

    @blk.vector
    def _(v):
        p.wait(v, "VE")
        p.vstep(v, v.memset(big, BIGV))
    ve_big = p.n("VE")

    @blk.gpsimd
    def _(g):
        g.wait_ge(p._sems["VE"], ve_big)
        for k in ("SC", "MM"):
            if p.c[k]:
                g.wait_ge(p._sems[k], p.c[k])
        p.inc(g.dma_start(out=p.IDX.rearrange("(a n) o -> a (n o)", a=128), in_=big),
              "LD", 16)
        g.wait_ge(p._sems["LD"], p.c["LD"])
    ld_init = p.n("LD")

    @blk.tensor
    def _(t):
        t.wait_ge(p._sems["VE"], p.xn_ve)
        t.wait_ge(p._sems["LC"], p.consts_lc)
        for tc in range(NTC):
            i1 = t.matmul(pcnt[:, tc * 16:(tc + 1) * 16], p.ones_cf[:],
                          p.sel_all[:, tc * 16:(tc + 1) * 16], start=True, stop=True)
        p.inc(i1, "MM")
    mm_cnt = p.n("MM")

    @blk.vector
    def _(v):
        v.wait_ge(p._sems["MM"], mm_cnt)
        v.wait_ge(p._sems["VE"], ve_big)
        p.vstep(v, v.tensor_copy(offrow[0:1, 0:16], p.erow_sb[:]))
        for tc in range(1, NTC):
            p.vstep(v, v.scalar_tensor_tensor(
                offrow[0:1, tc * 16:(tc + 1) * 16],
                pcnt[0:1, (tc - 1) * 16: tc * 16], 1.0,
                offrow[0:1, (tc - 1) * 16: tc * 16], ALU.mult, ALU.add))
    ve_off = p.n("VE")

    @blk.tensor
    def _(t):
        t.wait_ge(p._sems["VE"], ve_off)
        for tc in range(NTC):
            t.matmul(pall[:, tc * 16:(tc + 1) * 16], p.ut_sb[:],
                     p.sel_all[:, tc * 16:(tc + 1) * 16], start=True, stop=False)
            i1 = t.matmul(pall[:, tc * 16:(tc + 1) * 16], p.ones_rf[:],
                          offrow[0:1, tc * 16:(tc + 1) * 16], start=False, stop=True)
        p.inc(i1, "MM")
    mm_pos = p.n("MM")

    @blk.vector
    def _(v):
        v.wait_ge(p._sems["MM"], mm_pos)
        for tc in range(NTC):
            p.vstep(v, v.tensor_scalar(atmp, p.sel_all[:, tc * 16:(tc + 1) * 16],
                                       -BIGV, BIGV, ALU.mult, ALU.add))
            p.vstep(v, v.tensor_add(posm, atmp, pall[:, tc * 16:(tc + 1) * 16]))
            p.vstep(v, v.tensor_copy(posi_all[:, tc * 16:(tc + 1) * 16], posm))
    ve_pos = p.n("VE")

    @blk.gpsimd
    def _(g):
        g.wait_ge(p._sems["VE"], ve_pos)
        g.wait_ge(p._sems["LD"], ld_init)
        p.bc_et = g.alloc_register("bc_et")
        g.reg_mov(p.bc_et, E * T - 1)
        p.bc_t = g.alloc_register("bc_t")
        g.reg_mov(p.bc_t, T - 1)

    def mk(tc):
        def emit():
            @blk.gpsimd
            def _(g):
                for e in range(16):
                    p.inc(g.indirect_dma_start(
                        p.IDX,
                        IndirectOffsetOnAxis(posi_all[:, tc * 16 + e: tc * 16 + e + 1], 0),
                        p.iota16_sb[:, tc: tc + 1], None, compute_op=ALU.bypass,
                        bounds_check=p.bc_et, oob_is_err=False), "LD", 16)
                g.wait_ge(p._sems["LD"], p.c["LD"])
        return emit
    p.idx_scatters = [mk(tc) for tc in range(NTC)]
    ctx.close()


def emit_store_acc(p, d_y):
    blk = p.block

    @blk.sync
    def _(y):
        for k in ("VE", "SC", "MM", "LD", "LI", "LC", "LX"):
            if p.c[k]:
                p.wait(y, k)
        p.inc(y.dma_start(out=d_y, in_=p.ACC), "OUT", 16)
        p.wait(y, "OUT")


# version 15
# speedup vs baseline: 1.0819x; 1.0819x over previous
import numpy as np
from contextlib import ExitStack
import concourse.bass as bass
import concourse.mybir as mybir
from concourse.bass_utils import run_bass_kernel_spmd
from concourse.bass_core_rust import IndirectOffsetOnAxis

F32 = mybir.dt.float32
BF16 = mybir.dt.bfloat16
AF = mybir.ActivationFunctionType
ALU = mybir.AluOpType
AX = mybir.AxisListType

T = 2048
D = 1024
NTC = T // 128
MASKV = -1e9


def _bf16(a):
    import ml_dtypes
    return np.asarray(a).astype(ml_dtypes.bfloat16)


def host_consts():
    ident = np.eye(128, dtype=np.float32)
    g = np.arange(896)[None, :]
    pp = np.arange(128)[:, None]
    band = np.where(g >= pp + 384, 0.0, MASKV).astype(np.float32)
    ut = (np.arange(128)[:, None] < np.arange(128)[None, :]).astype(np.float32)
    iota = np.arange(128, dtype=np.float32).reshape(128, 1)
    erow = (np.arange(16, dtype=np.float32) * 2048).reshape(1, 16)
    iota16 = (np.arange(16)[None, :] * 128 + np.arange(128)[:, None]).astype(np.float32)
    return {"ident_f": ident, "ident_b": _bf16(ident), "band_b": _bf16(band),
            "ut": ut, "iota": iota, "erow": erow, "iota16": iota16}


class Prog:
    def __init__(self, name="prog"):
        self.nc = bass.Bass("TRN2", target_bir_lowering=False)
        self.es = ExitStack()
        self.es.__enter__()
        self._sems = {}
        self.c = {}
        self.inputs = {}
        self.sharded = {}
        self.outputs = []
        self.block = None

    def din(self, name, arr, sharded=False):
        arr0 = arr[0] if sharded else arr
        dt = BF16 if arr0.dtype.name == "bfloat16" else F32
        ap = self.nc.dram_tensor(name, list(arr0.shape), dt, kind="ExternalInput").ap()
        if sharded:
            self.sharded[name] = arr
        else:
            self.inputs[name] = np.ascontiguousarray(arr0)
        return ap

    def dout(self, name, shape, dt=F32):
        self.outputs.append(name)
        return self.nc.dram_tensor(name, list(shape), dt, kind="ExternalOutput").ap()

    def sb(self, name, shape, dt):
        return self.es.enter_context(self.nc.sbuf_tensor(name, shape, dt))

    def psum(self, name, shape, dt=F32):
        return self.es.enter_context(self.nc.psum_tensor(name, shape, dt))

    def sem(self, name):
        if name not in self._sems:
            self._sems[name] = self.es.enter_context(self.nc.semaphore(name))
            self.c[name] = 0
        return self._sems[name]

    def inc(self, inst, name, k=1):
        inst.then_inc(self._sems[name], k)
        self.c[name] += k
        return self.c[name]

    def n(self, name):
        return self.c[name]

    def wait(self, eng, name, val=None):
        eng.wait_ge(self._sems[name], self.c[name] if val is None else val)

    def vstep(self, v, inst):
        self.inc(inst, "VE")
        v.wait_ge(self._sems["VE"], self.c["VE"])

    def sstep(self, s, inst):
        self.inc(inst, "SC")
        s.wait_ge(self._sems["SC"], self.c["SC"])

    def vinc(self, inst):
        return self.inc(inst, "VE")

    def xadd(self, v, idx, emit_fn):
        if self.xw[idx]:
            v.wait_ge(self._sems["VE"], self.xw[idx])
        self.vinc(emit_fn())
        self.xw[idx] = self.c["VE"]

    def sinc(self, inst):
        return self.inc(inst, "SC")

    def start(self):
        for s in ("LD", "LI", "LC", "LX", "MM", "SC", "VE", "GF", "OUT"):
            self.sem(s)
        self.block = self.es.enter_context(self.nc.Block())

    def done(self):
        self.es.__exit__(None, None, None)

    def run(self, n_cores=8, **kw):
        in_maps = []
        for ci in range(n_cores):
            m = dict(self.inputs)
            for k, v in self.sharded.items():
                m[k] = v[ci]
            in_maps.append(m)
        return run_bass_kernel_spmd(self.nc, in_maps, core_ids=list(range(n_cores)), **kw)



def emit_consts(p, cd):
    p.ident_f = p.sb("ident_f", [128, 128], F32)
    p.ident_b = p.sb("ident_b", [128, 128], BF16)
    p.band_b = p.sb("band_b", [128, 896], BF16)
    p.ones_rb = p.sb("ones_rb", [1, 512], BF16)
    p.ones_cb = p.sb("ones_cb", [128, 1], BF16)
    p.ones_rf = p.sb("ones_rf", [1, 128], F32)
    p.ones_rbig = p.sb("ones_rbig", [1, 128], BF16)
    p.eps_col = p.sb("eps_col", [128, 1], F32)
    p.ones_cf = p.sb("ones_cf", [128, 1], F32)
    p.ut_sb = p.sb("ut_sb", [128, 128], F32)
    p.iota_sb = p.sb("iota_sb", [128, 1], F32)
    p.erow_sb = p.sb("erow_sb", [1, 16], F32)
    p.iota16_sb = p.sb("iota16_sb", [128, 16], F32)
    d_if = p.din("c_ident_f", cd["ident_f"])
    d_ib = p.din("c_ident_b", cd["ident_b"])
    d_bb = p.din("c_band_b", cd["band_b"])
    d_ut = p.din("c_ut", cd["ut"])
    d_io = p.din("c_iota", cd["iota"])
    d_er = p.din("c_erow", cd["erow"])
    d_i16 = p.din("c_iota16", cd["iota16"])
    blk = p.block

    @blk.sync
    def _(y):
        p.inc(y.dma_start(out=p.ident_f[:], in_=d_if[:]), "LC", 16)
        p.inc(y.dma_start(out=p.ident_b[:], in_=d_ib[:]), "LC", 16)
        p.inc(y.dma_start(out=p.band_b[:], in_=d_bb[:]), "LC", 16)
        p.inc(y.dma_start(out=p.ut_sb[:], in_=d_ut[:]), "LC", 16)
        p.inc(y.dma_start(out=p.iota_sb[:], in_=d_io[:]), "LC", 16)
        p.inc(y.dma_start(out=p.erow_sb[:], in_=d_er[:]), "LC", 16)
        p.inc(y.dma_start(out=p.iota16_sb[:], in_=d_i16[:]), "LC", 16)

    @blk.vector
    def _(v):
        p.vstep(v, v.memset(p.ones_rb[:], 1.0))
        p.vstep(v, v.memset(p.ones_cb[:], 1.0))
        p.vstep(v, v.memset(p.ones_rf[:], 1.0))
        p.vstep(v, v.memset(p.ones_rbig[:], 1.0))
        p.vstep(v, v.memset(p.eps_col[:], 1e-5))
        p.vstep(v, v.memset(p.ones_cf[:], 1.0))
    p.consts_lc = p.n("LC")
    p.consts_ve = p.n("VE")


def emit_load_x(p, d_x):
    p.x_tm = p.sb("x_tm", [128, NTC * 1024], F32)
    blk = p.block

    @blk.sync
    def _(y):
        src = d_x.rearrange("(j p) d -> p j d", p=128)
        dst = p.x_tm[:].rearrange("p (j d) -> p j d", j=NTC)
        p.inc(y.dma_start(out=dst, in_=src), "LX", 16)
        y.wait_ge(p._sems["LX"], p.c["LX"])
    p.x_li = p.n("LX")
    p.x_ready = [("LX", p.x_li)]
    p.xw = [0] * 32


def emit_store_x(p, d_y, src_sb):
    blk = p.block

    @blk.sync
    def _(y):
        for k in ("VE", "SC", "MM", "LD", "LI", "LC", "LX"):
            if p.c[k]:
                p.wait(y, k)
        dst = d_y.rearrange("(j p) d -> p j d", p=128)
        src = src_sb[:].rearrange("p (j d) -> p j d", j=NTC)
        p.inc(y.dma_start(out=dst, in_=src), "OUT", 16)
        p.wait(y, "OUT")


def emit_barrier(p):
    blk = p.block
    snap = {k: p.c[k] for k in ("LD", "LI", "LC", "LX", "MM", "SC", "VE", "GF")}

    def waits(e):
        for k, v in snap.items():
            if v:
                e.wait_ge(p._sems[k], v)

    @blk.tensor
    def _(t):
        waits(t)

    @blk.scalar
    def _(s):
        waits(s)

    @blk.vector
    def _(v):
        waits(v)

    @blk.gpsimd
    def _(g):
        waits(g)



def alloc_ln(p):
    p.xn_fm = p.sb("xn_fm", [128, 8 * 2048], BF16)
    p.xsq = af32(p, 77824, 1024)
    p.xnf = af32(p, 81920, 1024)
    p.xnT = af32(p, 86016, 1024)
    p.xnb = p.sb("xnb", [128, 1024], BF16)
    p.negm8 = p.sb("negm8", [128, 16], F32)
    p.maxall = p.sb("maxall", [128, 16], F32)
    p.m2sb = p.sb("m2sb", [1, 2], F32)
    p.m2bf = p.sb("m2bf", [1, 1], BF16)
    p.lncol = p.sb("lncol", [128, 64], F32)
    p.rexp = p.sb("rexp", [128, 16], F32)
    p.rsel = p.sb("rsel", [128, 16], F32)
    p.rsel2 = p.sb("rsel2", [128, 16], F32)
    p.rtmp = p.sb("rtmp", [128, 16], F32)
    p.rcol = p.sb("rcol", [128, 4], F32)
    p.gw_sb = p.sb("gw_sb", [128, 8 * 16], F32)
    p.rb_sb = p.sb("rb_sb", [1, 16], F32)
    p.C_tm = {}


def emit_ln_router(p, tag, E, K, d_gwT, d_rb, sparse=False):
    blk = p.block
    if sparse:
        p.sel_all = p.sb("sel_all_" + tag, [128, NTC * E], F32)
        p.xn_hbm = p.nc.dram_tensor("xn_hbm_" + tag, [T, D], BF16, kind="Internal").ap()
        p.C_hbm = p.nc.dram_tensor("C_hbm_" + tag, [T * E, 1], F32, kind="Internal").ap()
    p.C_tm[tag] = p.sb("C_tm_" + tag, [128, NTC * E], F32)
    C_tm = p.C_tm[tag]
    if tag == "L1":
        p.CT_sb = p.sb("CT_sb", [16, 2048], BF16)
        p.cbf = p.sb("cbf", [128, 16], BF16)
    s1a = p.lncol[:, 0:16]; s2a = p.lncol[:, 16:32]
    rstda = p.lncol[:, 32:48]; negma = p.lncol[:, 48:64]
    m_ = p.rcol[:, 0:1]; negm_ = p.rcol[:, 1:2]; sum_ = p.rcol[:, 2:3]; rec_ = p.rcol[:, 3:4]

    @blk.sync
    def _(y):
        if p.c["MM"] > 0:
            p.wait(y, "MM")
        src = d_gwT.rearrange("(c p) e -> p c e", p=128)
        dst = p.gw_sb[:, 0:8 * E].rearrange("p (c e) -> p c e", c=8)
        p.inc(y.dma_start(out=dst, in_=src), "LI", 16)
        if d_rb is not None:
            p.inc(y.dma_start(out=p.rb_sb[:, 0:E], in_=d_rb), "LI", 16)
        y.wait_ge(p._sems["LI"], p.c["LI"])
    li_w = p.n("LI")

    with p.nc.psum_tensor("psT_b_" + tag, [128, 1024], BF16) as psT_b, \
         p.nc.psum_tensor("psT_f_" + tag, [128, 1024], F32) as psT_f, \
         p.nc.psum_tensor("ps_r_" + tag, [128, 16], F32) as ps_r:
        @blk.vector
        def _(v):
            for sk, sv in p.x_ready:
                v.wait_ge(p._sems[sk], sv)
            if getattr(p, "ln2_war_ve", 0):
                v.wait_ge(p._sems["VE"], p.ln2_war_ve)
            for tc in range(NTC):
                p.vinc(v.reduce_sum(s1a[:, tc:tc + 1],
                                    p.x_tm[:, tc * 1024:(tc + 1) * 1024], axis=AX.X))
        ve_s1 = p.n("VE")

        @blk.scalar
        def _(s):
            for sk, sv in p.x_ready:
                s.wait_ge(p._sems[sk], sv)
            if getattr(p, "ln2_war_ve", 0):
                s.wait_ge(p._sems["VE"], p.ln2_war_ve)
            sc00 = p.n("SC")
            dumps = [p.xsq, p.xnf]
            for tc in range(NTC):
                if tc >= 2:
                    s.wait_ge(p._sems["SC"], sc00 + tc - 1)
                p.sinc(s.activation(dumps[tc % 2][:], p.x_tm[:, tc * 1024:(tc + 1) * 1024],
                                    AF.Square, accum_out=s2a[:, tc:tc + 1]))
        sc_s2 = p.n("SC")

        @blk.vector
        def _(v):
            v.wait_ge(p._sems["VE"], ve_s1)
            v.wait_ge(p._sems["SC"], sc_s2)
            p.vstep(v, v.tensor_scalar(s1a, s1a, 1.0 / D, None, ALU.mult))
            p.vstep(v, v.tensor_scalar(s2a, s2a, 1.0 / D, None, ALU.mult))
            p.vstep(v, v.scalar_tensor_tensor(rstda, s1a, -1.0, s1a, ALU.mult, ALU.mult))
            p.vstep(v, v.tensor_add(rstda, rstda, s2a))
        ve_var = p.n("VE")

        @blk.scalar
        def _(s):
            s.wait_ge(p._sems["VE"], ve_var)
            p.sstep(s, s.activation(rstda, rstda, AF.Ln, bias=p.eps_col[:]))
            p.sstep(s, s.activation(rstda, rstda, AF.Exp, scale=-0.5))
        sc_sq = p.n("SC")

        @blk.vector
        def _(v):
            v.wait_ge(p._sems["SC"], sc_sq)
            p.vstep(v, v.scalar_tensor_tensor(negma, s1a, -1.0, rstda, ALU.mult, ALU.mult))
        ve_nm = p.n("VE")
        p.ln2_war_ve = ve_nm

        for tc in range(NTC):
            x_c = p.x_tm[:, tc * 1024:(tc + 1) * 1024]
            rstd = rstda[:, tc:tc + 1]
            negmr = negma[:, tc:tc + 1]
            @blk.scalar
            def _(s):
                s.wait_ge(p._sems["VE"], ve_nm)
                if tc > 0:
                    s.wait_ge(p._sems["MM"], p.ln_mm_tp)
                if sparse and tc > 0:
                    s.wait_ge(p._sems["LI"], p.ln_xns)
                p.sstep(s, s.activation(p.xnb[:], x_c, AF.Identity, bias=negmr, scale=rstd))
                p.sstep(s, s.activation(p.xnf[:], x_c, AF.Identity, bias=negmr, scale=rstd))
            sc_xn = p.n("SC")

            if sparse:
                @blk.sync
                def _(y):
                    y.wait_ge(p._sems["SC"], sc_xn)
                    p.inc(y.dma_start(
                        out=p.xn_hbm[tc * 128:(tc + 1) * 128, :], in_=p.xnb[:]), "LI", 16)
                    y.wait_ge(p._sems["LI"], p.c["LI"])
                p.ln_xns = p.n("LI")

            @blk.tensor
            def _(t):
                t.wait_ge(p._sems["SC"], sc_xn)
                t.wait_ge(p._sems["LC"], p.consts_lc)
                if tc > 0:
                    t.wait_ge(p._sems["VE"], p.ln_ve_evac)
                for b in range(8):
                    i1 = t.transpose(psT_b[:, b * 128:(b + 1) * 128],
                                     p.xnb[:, b * 128:(b + 1) * 128], p.ident_b[:])
                p.inc(i1, "MM")
                for b in range(8):
                    i2 = t.transpose(psT_f[:, b * 128:(b + 1) * 128],
                                     p.xnf[:, b * 128:(b + 1) * 128], p.ident_f[:])
                p.inc(i2, "MM")
            p.ln_mm_tp = p.n("MM")
            mm_tp = p.n("MM")

            @blk.vector
            def _(v):
                v.wait_ge(p._sems["MM"], mm_tp)
                dst = p.xn_fm[:].rearrange("p (c n) -> p c n", c=8)[:, :, tc * 128:(tc + 1) * 128]
                p.vstep(v, v.tensor_copy(dst, psT_b[:].rearrange("p (c n) -> p c n", c=8)))
                p.vstep(v, v.tensor_copy(p.xnT[:], psT_f[:]))
            p.ln_ve_evac = p.n("VE")
            ve_ev = p.n("VE")

            @blk.tensor
            def _(t):
                t.wait_ge(p._sems["VE"], ve_ev)
                t.wait_ge(p._sems["LI"], li_w)
                if tc > 0:
                    t.wait_ge(p._sems["SC"], p.ln_sc_rexp)
                for b in range(8):
                    i1 = t.matmul(ps_r[:, 0:E], p.xnT[:, b * 128:(b + 1) * 128],
                                  p.gw_sb[:, b * E:(b + 1) * E], start=(b == 0),
                                  stop=(b == 7 and d_rb is None))
                if d_rb is not None:
                    i1 = t.matmul(ps_r[:, 0:E], p.ones_rf[:], p.rb_sb[:, 0:E],
                                  start=False, stop=True)
                p.inc(i1, "MM")
            mm_r = p.n("MM")

            @blk.vector
            def _(v):
                v.wait_ge(p._sems["MM"], mm_r)
                p.vstep(v, v.reduce_max(m_, ps_r[:, 0:E], axis=AX.X))
                p.vstep(v, v.tensor_scalar(negm_, m_, -1.0, None, ALU.mult))
            ve_m = p.n("VE")

            @blk.scalar
            def _(s):
                s.wait_ge(p._sems["VE"], ve_m)
                p.sstep(s, s.activation(p.rexp[:, 0:E], ps_r[:, 0:E], AF.Exp, bias=negm_))
            p.ln_sc_rexp = p.n("SC")
            sc_e = p.n("SC")

            @blk.vector
            def _(v):
                v.wait_ge(p._sems["SC"], sc_e)
                cur = p.rtmp[:, 0:E]
                p.vstep(v, v.tensor_copy(cur, p.rexp[:, 0:E]))
                p.vstep(v, v.memset(p.rsel[:, 0:E], 0.0))
                for j in range(K):
                    p.vstep(v, v.reduce_max(m_, cur, axis=AX.X))
                    p.vstep(v, v.tensor_scalar(p.rsel2[:, 0:E], cur, m_, None, ALU.is_equal))
                    p.vstep(v, v.tensor_add(p.rsel[:, 0:E], p.rsel[:, 0:E], p.rsel2[:, 0:E]))
                    if j < K - 1:
                        p.vstep(v, v.tensor_scalar(p.rsel2[:, 0:E], p.rsel2[:, 0:E],
                                                   -1.0, 1.0, ALU.mult, ALU.add))
                        p.vstep(v, v.tensor_mul(cur, cur, p.rsel2[:, 0:E]))
                if sparse:
                    p.vstep(v, v.tensor_copy(p.sel_all[:, tc * E:(tc + 1) * E],
                                             p.rsel[:, 0:E]))
                p.vstep(v, v.tensor_mul(p.rtmp[:, 0:E], p.rexp[:, 0:E], p.rsel[:, 0:E]))
                p.vstep(v, v.reduce_sum(sum_, p.rtmp[:, 0:E], axis=AX.X))
                p.vstep(v, v.reciprocal(rec_, sum_))
                p.vstep(v, v.tensor_scalar(C_tm[:, tc * E:(tc + 1) * E], p.rtmp[:, 0:E],
                                           rec_, None, ALU.mult))
                if tag == "L1":
                    p.vstep(v, v.tensor_copy(p.cbf[:, 0:E], C_tm[:, tc * E:(tc + 1) * E]))
            ve_c = p.n("VE")

            if tag == "L1":
                @blk.tensor
                def _(t):
                    t.wait_ge(p._sems["VE"], ve_c)
                    p.inc(t.transpose(psT_b[0:16, 0:128], p.cbf[:, 0:16], p.ident_b[:]), "MM")
                mm_ct = p.n("MM")

                @blk.vector
                def _(v):
                    v.wait_ge(p._sems["MM"], mm_ct)
                    p.vstep(v, v.tensor_copy(p.CT_sb[0:16, tc * 128:(tc + 1) * 128],
                                             psT_b[0:16, 0:128]))
                p.ln_ve_evac = p.n("VE")
            p.ln_ve_done = p.n("VE")
    if sparse:
        @blk.sync
        def _(y):
            p.wait(y, "VE")
            sc_ap = p.C_tm[tag][:].rearrange("p (j e) -> p j e", j=NTC)
            dst = p.C_hbm.rearrange("(j p e) o -> p j (e o)", j=NTC, p=128)
            p.inc(y.dma_start(out=dst, in_=sc_ap), "LI", 16)
            y.wait_ge(p._sems["LI"], p.c["LI"])
        p.ln_cs = p.n("LI")
    p.xn_ve = p.n("VE")
    p.xn_mm = p.n("MM")



ARENA_B = 88 * 1024


def alloc_arena(p):
    p.arena = p.sb("arena", [128, ARENA_B // 2], BF16)


def abf(p, off_b, n, parts=128):
    return p.arena[0:parts, off_b // 2: off_b // 2 + n]


def af32(p, off_b, n, parts=128):
    return p.arena[:].bitcast(F32)[0:parts, off_b // 4: off_b // 4 + n]


class SegStream:

    def __init__(self, p, name, slot_views, srcs):
        self.p, self.name = p, name
        self.slots = slot_views
        self.srcs = srcs
        self.ld = {}

    def _load(self, g, seg, gate_mm, gate_all=False):
        p = self.p
        dst, src = self.srcs[seg](self.slots[seg % len(self.slots)])
        self.ld[seg] = _gload(p, g, [(dst, src)], gate_mm=gate_mm, gate_all=gate_all)

    def prime(self, gate_mm=0):
        p, blk = self.p, self.p.block

        @blk.gpsimd
        def _(g):
            for s in range(min(len(self.slots), len(self.srcs))):
                self._load(g, s, gate_mm, gate_all=True)

    def done(self, seg, mm_count):
        nxt = seg + len(self.slots)
        if nxt < len(self.srcs):
            p, blk = self.p, self.p.block

            @blk.gpsimd
            def _(g):
                self._load(g, nxt, mm_count)

    def use(self, seg):
        return self.slots[seg % len(self.slots)], self.ld[seg]


def _gload(p, g, dmas, gate_mm=0, gate_all=False):
    if gate_mm > 0:
        g.wait_ge(p._sems["MM"], gate_mm)
    if gate_all:
        for k in ("SC", "VE"):
            if p.c[k]:
                g.wait_ge(p._sems[k], p.c[k])
    for dst, src in dmas:
        p.inc(g.dma_start(out=dst, in_=src), "LD", 16)
    cum = p.n("LD")
    g.wait_ge(p._sems["LD"], cum)
    return cum


def _r3(ap, c):
    return ap.rearrange("p (c n) -> p c n", c=c)



def emit_ffn_l1(p, d_w1, d_b1fm, d_w2, d_b2):
    blk = p.block
    E = 16
    NT = 1024
    b1_sb = af32(p, 88064, 16 * E)
    b2_sb = abf(p, 86016, 1024, parts=16)
    w1slots = [abf(p, 0, 4096), abf(p, 8192, 4096)]
    w2slots = [abf(p, 16384, 8192), abf(p, 32768, 8192)]
    hmid = abf(p, 49152, 16 * NT)
    ctx = ExitStack()
    ph = ctx.enter_context(p.nc.psum_tensor("ph_l1", [128, 2048], F32))
    po = ctx.enter_context(p.nc.psum_tensor("po_l1", [128, 1024], F32))
    pcb = ctx.enter_context(p.nc.psum_tensor("pcb_l1", [128, 512], F32))

    @blk.sync
    def _(y):
        p.wait(y, "MM")
        p.inc(y.dma_start(out=b1_sb[:], in_=d_b1fm[:]), "LI", 16)
        p.wait(y, "LI")
    li_b1 = p.n("LI")

    @blk.gpsimd
    def _(g):
        _gload(p, g, [(b2_sb[:], d_b2[:])], gate_mm=p.n("MM"), gate_all=True)
    ld_b2 = p.n("LD")

    seq = [(tt, e) for tt in range(2) for e in range(E)]

    def w1src(tt, e, q):
        def f(slot):
            return (_r3(slot, 8), d_w1[e, :, q])
        return f

    def w2src(tt, e, h):
        def f(slot):
            return (_r3(slot, 16), d_w2[e, :, h])
        return f

    s_w1 = SegStream(p, "w1", w1slots, [w1src(tt, e, q) for tt, e in seq for q in range(4)])
    s_w2 = SegStream(p, "w2", w2slots, [w2src(tt, e, h) for tt, e in seq for h in range(2)])
    s_w1.prime(gate_mm=p.n("MM"))
    s_w2.prime(gate_mm=p.n("MM"))

    cb_prev = 0
    for tc in range(NTC):
        for h in range(2):
            @blk.tensor
            def _(t):
                t.wait_ge(p._sems["VE"], p.xn_ve)
                t.wait_ge(p._sems["LD"], ld_b2)
                if cb_prev:
                    t.wait_ge(p._sems["VE"], cb_prev)
                p.inc(t.matmul(pcb[:], p.CT_sb[0:16, tc * 128:(tc + 1) * 128],
                               b2_sb[0:16, h * 512:(h + 1) * 512],
                               start=True, stop=True), "MM")
            mm_cb = p.n("MM")

            @blk.vector
            def _(v):
                v.wait_ge(p._sems["MM"], mm_cb)
                xs = p.x_tm[:, tc * 1024 + h * 512: tc * 1024 + (h + 1) * 512]
                p.xadd(v, tc * 2 + h, lambda: v.tensor_add(xs, xs, pcb[:]))
            cb_prev = p.n("VE")

    gel = {}
    pove = {}
    jg = 0
    npo = 0
    for i, (tt, e) in enumerate(seq):
        prev_b_mm = p.n("MM")
        jg0 = jg
        sc0 = p.n("SC")
        mm_as = {}
        for ic in range(16):
            q = ic // 4
            w1buf, w1ld = s_w1.use(i * 4 + q)

            @blk.tensor
            def _(t):
                t.wait_ge(p._sems["LD"], w1ld)
                if i == 0 and ic == 0:
                    t.wait_ge(p._sems["VE"], p.xn_ve)
                if jg >= 2:
                    t.wait_ge(p._sems["SC"], gel[jg - 2])
                for n in range(2):
                    for dc in range(8):
                        i1 = t.matmul(ph[:, (jg % 2) * 1024 + n * 512:(jg % 2) * 1024 + n * 512 + 512],
                                      w1buf[:, dc * 512 + (ic % 4) * 128: dc * 512 + (ic % 4) * 128 + 128],
                                      p.xn_fm[:, dc * 2048 + tt * NT + n * 512: dc * 2048 + tt * NT + n * 512 + 512],
                                      start=(dc == 0), stop=(dc == 7))
                p.inc(i1, "MM")
            mm_as[ic] = p.n("MM")
            gel[jg] = sc0 + ic + 1
            jg += 1
            if ic % 4 == 3:
                s_w1.done(i * 4 + q, mm_as[ic])

        @blk.scalar
        def _(s):
            for ic in range(16):
                s.wait_ge(p._sems["MM"], mm_as[ic])
                if ic == 0:
                    s.wait_ge(p._sems["MM"], prev_b_mm)
                    s.wait_ge(p._sems["LI"], li_b1)
                p.sinc(s.activation(hmid[:, ic * NT:(ic + 1) * NT],
                                    ph[:, ((jg0 + ic) % 2) * 1024:((jg0 + ic) % 2) * 1024 + 1024],
                                    AF.Gelu, bias=b1_sb[:, e * 16 + ic: e * 16 + ic + 1]))
        assert p.n("SC") == sc0 + 16
        sc_a_done = p.n("SC")

        for h in range(2):
            w2buf, w2ld = s_w2.use(i * 2 + h)
            for ts in range(8):
                @blk.tensor
                def _(t):
                    t.wait_ge(p._sems["LD"], w2ld)
                    if h == 0 and ts == 0:
                        t.wait_ge(p._sems["SC"], sc_a_done)
                    if npo >= 2:
                        t.wait_ge(p._sems["VE"], pove[npo - 2])
                    for ic in range(16):
                        i1 = t.matmul(po[:, (npo % 2) * 512:(npo % 2) * 512 + 512],
                                      hmid[:, ic * NT + ts * 128: ic * NT + ts * 128 + 128],
                                      w2buf[:, ic * 512:(ic + 1) * 512],
                                      start=(ic == 0), stop=(ic == 15))
                    p.inc(i1, "MM")
                mm_b = p.n("MM")

                @blk.vector
                def _(v):
                    v.wait_ge(p._sems["MM"], mm_b)
                    tc = tt * 8 + ts
                    c_col = p.C_tm["L1"][:, tc * 16 + e: tc * 16 + e + 1]
                    xs = p.x_tm[:, tc * 1024 + h * 512: tc * 1024 + h * 512 + 512]
                    po_s = po[:, (npo % 2) * 512:(npo % 2) * 512 + 512]
                    p.xadd(v, tc * 2 + h, lambda: v.scalar_tensor_tensor(
                        xs, po_s, c_col, xs, ALU.mult, ALU.add))
                pove[npo] = p.n("VE")
                npo += 1
            s_w2.done(i * 2 + h, p.n("MM"))
    p.ffn1_mm = p.n("MM")
    ctx.close()


def emit_ffn_l0(p, d_gup, d_dp):
    blk = p.block
    E = 8
    NT = 1024
    gu_slots = [abf(p, 0, 4096), abf(p, 8192, 4096),
                abf(p, 16384, 4096), abf(p, 24576, 4096)]
    dn_slots = [abf(p, 32768, 4096), abf(p, 40960, 4096)]
    act = abf(p, 49152, 8 * NT)
    sg = [abf(p, 65536, NT), abf(p, 67584, NT)]
    ctx = ExitStack()
    phg = ctx.enter_context(p.nc.psum_tensor("phg_l0", [128, 2048], F32))
    phu = ctx.enter_context(p.nc.psum_tensor("phu_l0", [128, 1024], F32))
    po = ctx.enter_context(p.nc.psum_tensor("po_l0", [128, 1024], F32))

    seq = [(tt, e) for tt in range(2) for e in range(E)]

    def gusrc(e, q):
        def f(slot):
            return (_r3(slot, 8), d_gup[e, :, q])
        return f

    def dnsrc(e, h):
        def f(slot):
            return (_r3(slot, 8), d_dp[e, :, h])
        return f

    s_gu = SegStream(p, "gu", gu_slots, [gusrc(e, q) for tt, e in seq for q in range(4)])
    s_dn = SegStream(p, "dn", dn_slots, [dnsrc(e, h) for tt, e in seq for h in range(2)])
    s_gu.prime(gate_mm=p.n("MM"))
    s_dn.prime(gate_mm=p.n("MM"))

    sil = {}
    mul = {}
    pove = {}
    jg = 0
    npo = 0
    for i, (tt, e) in enumerate(seq):
        prev_b_mm = p.n("MM")
        jg0 = jg
        sil0 = p.n("SC")
        mul0 = p.n("VE")
        mm_gs = {}
        mm_us = {}
        for icq in range(8):
            gseg = i * 4 + (icq // 4) * 2
            useg = gseg + 1
            gbuf, gld = s_gu.use(gseg)
            ubuf, uld = s_gu.use(useg)

            @blk.tensor
            def _(t):
                t.wait_ge(p._sems["LD"], gld)
                if i == 0 and icq == 0:
                    t.wait_ge(p._sems["VE"], p.xn_ve)
                if jg >= 2:
                    t.wait_ge(p._sems["SC"], sil[jg - 2])
                for dc in range(8):
                    for n in range(2):
                        i1 = t.matmul(phg[:, (jg % 2) * 1024 + n * 512:(jg % 2) * 1024 + n * 512 + 512],
                                      gbuf[:, dc * 512 + (icq % 4) * 128: dc * 512 + (icq % 4) * 128 + 128],
                                      p.xn_fm[:, dc * 2048 + tt * NT + n * 512: dc * 2048 + tt * NT + n * 512 + 512],
                                      start=(dc == 0), stop=(dc == 7))
                p.inc(i1, "MM")
                t.wait_ge(p._sems["LD"], uld)
                if jg >= 1:
                    t.wait_ge(p._sems["VE"], mul[jg - 1])
                for dc in range(8):
                    for n in range(2):
                        i1 = t.matmul(phu[:, n * 512:(n + 1) * 512],
                                      ubuf[:, dc * 512 + (icq % 4) * 128: dc * 512 + (icq % 4) * 128 + 128],
                                      p.xn_fm[:, dc * 2048 + tt * NT + n * 512: dc * 2048 + tt * NT + n * 512 + 512],
                                      start=(dc == 0), stop=(dc == 7))
                p.inc(i1, "MM")
            mm_gs[icq] = p.n("MM") - 1
            mm_us[icq] = p.n("MM")
            if icq == 3 or icq == 7:
                s_gu.done(gseg, p.n("MM"))
                s_gu.done(useg, p.n("MM"))
            sil[jg] = sil0 + icq + 1
            mul[jg] = mul0 + icq + 1
            jg += 1

        @blk.scalar
        def _(s):
            for icq in range(8):
                j = jg0 + icq
                s.wait_ge(p._sems["MM"], mm_gs[icq])
                if j >= 2:
                    s.wait_ge(p._sems["VE"], mul[j - 2])
                p.sinc(s.activation(sg[j % 2][:], phg[:, (j % 2) * 1024:(j % 2) * 1024 + 1024],
                                    AF.Silu))
        assert p.n("SC") == sil0 + 8

        @blk.vector
        def _(v):
            for icq in range(8):
                j = jg0 + icq
                v.wait_ge(p._sems["SC"], sil[j])
                v.wait_ge(p._sems["MM"], mm_us[icq])
                if icq == 0:
                    v.wait_ge(p._sems["MM"], prev_b_mm)
                p.vinc(v.tensor_mul(act[:, icq * NT:(icq + 1) * NT], sg[j % 2][:], phu[:]))
        assert p.n("VE") == mul0 + 8
        ve_a_done = p.n("VE")

        for h in range(2):
            dbuf, dld = s_dn.use(i * 2 + h)
            for ts in range(8):
                @blk.tensor
                def _(t):
                    t.wait_ge(p._sems["LD"], dld)
                    if h == 0 and ts == 0:
                        t.wait_ge(p._sems["VE"], ve_a_done)
                    if npo >= 2:
                        t.wait_ge(p._sems["VE"], pove[npo - 2])
                    for qc in range(8):
                        i1 = t.matmul(po[:, (npo % 2) * 512:(npo % 2) * 512 + 512],
                                      act[:, qc * NT + ts * 128: qc * NT + ts * 128 + 128],
                                      dbuf[:, qc * 512:(qc + 1) * 512],
                                      start=(qc == 0), stop=(qc == 7))
                    p.inc(i1, "MM")
                mm_b = p.n("MM")

                @blk.vector
                def _(v):
                    v.wait_ge(p._sems["MM"], mm_b)
                    tc = tt * 8 + ts
                    c_col = p.C_tm["L0"][:, tc * 8 + e: tc * 8 + e + 1]
                    xs = p.x_tm[:, tc * 1024 + h * 512: tc * 1024 + h * 512 + 512]
                    po_s = po[:, (npo % 2) * 512:(npo % 2) * 512 + 512]
                    p.xadd(v, tc * 2 + h, lambda: v.scalar_tensor_tensor(
                        xs, po_s, c_col, xs, ALU.mult, ALU.add))
                pove[npo] = p.n("VE")
                npo += 1
            s_dn.done(i * 2 + h, p.n("MM"))
    p.ffn0_mm = p.n("MM")
    ctx.close()



def _norm_bound(p, ps_misc, qsq_src_q, qsq_src_k, qsq, h, n_heads_done_mm):
    blk = p.block
    qn = ps_misc[:, 0:32]
    tr = ps_misc[0:1, 64:192]
    src_ve = p.n("VE")

    @blk.vector
    def _(v):
        v.wait_ge(p._sems["MM"], n_heads_done_mm)
        v.wait_ge(p._sems["VE"], src_ve)
        if h > 0:
            v.wait_ge(p._sems["MM"], p.nb_mm)
        p.vstep(v, v.tensor_mul(qsq[0:64, :], qsq_src_q, qsq_src_q))
        p.vstep(v, v.tensor_mul(qsq[64:128, :], qsq_src_k, qsq_src_k))
    sc_sq = p.n("VE")

    @blk.tensor
    def _(t):
        t.wait_ge(p._sems["VE"], sc_sq)
        if h > 0:
            t.wait_ge(p._sems["VE"], p.nb_ve)
        for tc in range(NTC):
            i1 = t.matmul(qn[:, tc:tc + 1], qsq[0:64, tc * 128:(tc + 1) * 128],
                          p.ones_cb[0:64, :], start=True, stop=True)
            i2 = t.matmul(qn[:, 16 + tc:17 + tc], qsq[64:128, tc * 128:(tc + 1) * 128],
                          p.ones_cb[64:128, :], start=True, stop=True)
        p.inc(i2, "MM")
    p.nb_mm = p.n("MM")

    @blk.vector
    def _(v):
        v.wait_ge(p._sems["MM"], p.nb_mm)
        p.vstep(v, v.reduce_max(p.maxall[:, h:h + 1], qn, axis=AX.X))
    ve_mx = p.n("VE")

    @blk.tensor
    def _(t):
        t.wait_ge(p._sems["VE"], ve_mx)
        if h > 0:
            t.wait_ge(p._sems["VE"], p.nb_ve2)
        p.inc(t.transpose(tr, p.maxall[:, h:h + 1], p.ident_f[:]), "MM")
    mm_tr = p.n("MM")

    @blk.vector
    def _(v):
        v.wait_ge(p._sems["MM"], mm_tr)
        p.vstep(v, v.reduce_max(p.m2sb[:, 0:1], tr, axis=AX.X))
        p.vstep(v, v.tensor_scalar(p.m2sb[:, 1:2], p.m2sb[:, 0:1], -0.125, None, ALU.mult))
        p.vstep(v, v.tensor_copy(p.m2bf[:, 0:1], p.m2sb[:, 1:2]))
    p.nb_ve2 = p.n("VE")
    ve_bf = p.n("VE")

    @blk.tensor
    def _(t):
        t.wait_ge(p._sems["VE"], ve_bf)
        p.inc(t.matmul(ps_misc[:, 32:33], p.ones_rbig[:], p.m2bf[:, 0:1],
                       start=True, stop=True), "MM")
    mm_bc = p.n("MM")

    @blk.vector
    def _(v):
        v.wait_ge(p._sems["MM"], mm_bc)
        p.vstep(v, v.tensor_copy(p.negm8[:, h:h + 1], ps_misc[:, 32:33]))
    p.nb_ve = p.n("VE")


def _attn_head_tt(p, pp, psy, psbc, qap, kap, vhat, h, tt, NT, causal, yun, Ydst,
                  recl_f, recl_b, st, pkey="psyf"):
    blk = p.block
    nh = NT // 512
    sc_list = list(range(4 * tt + 4)) if causal and NT == 512 else list(range(16))
    n_sc = len(sc_list)
    negm8 = p.negm8[:, h:h + 1]
    j0 = st["j"]

    sc_base = p.n("SC")
    ve_base = p.n("VE")
    masked = {}
    nmask = 0
    for k_i, sc in enumerate(sc_list):
        if causal and 4 * tt <= sc <= 4 * tt + 3 and NT == 512:
            nmask += 1
            masked[k_i] = nmask
    exp_cnt = {k_i: sc_base + k_i + 1 for k_i in range(n_sc)}
    mm_s = {}

    @blk.tensor
    def _(t):
        def S(k_i):
            sc = sc_list[k_i]
            j = j0 + k_i
            if j >= 2:
                t.wait_ge(p._sems["SC"], st["exp"][j - 2] if j - 2 < j0
                          else exp_cnt[j - 2 - j0])
            if k_i == 0:
                for key, val in st.pop("first_waits", {}).items():
                    t.wait_ge(p._sems[key], val)
            tile = pp[:, (j % 2) * NT:(j % 2 + 1) * NT]
            for n in range(nh):
                i1 = t.matmul(tile[:, n * 512:(n + 1) * 512],
                              kap[:, sc * 128:(sc + 1) * 128],
                              qap[:, tt * NT + n * 512: tt * NT + n * 512 + 512],
                              start=True, stop=True)
            p.inc(i1, "MM")
            mm_s[k_i] = p.n("MM")

        def A(k_i):
            sc = sc_list[k_i]
            j = j0 + k_i
            t.wait_ge(p._sems["SC"], exp_cnt[k_i])
            if k_i == 0:
                t.wait_ge(p._sems["VE"], st.get(pkey, 0))
            par = j % 2
            for n in range(nh):
                i1 = t.matmul(psy[:, n * 512:(n + 1) * 512],
                              vhat[:, sc * 65:(sc + 1) * 65],
                              st["pt"][par][:, n * 512:(n + 1) * 512],
                              start=(k_i == 0), stop=(k_i == n_sc - 1))
            p.inc(i1, "MM")

        S(0)
        for k_i in range(1, n_sc):
            S(k_i)
            A(k_i - 1)
        A(n_sc - 1)

    if nmask:
        @blk.vector
        def _(v):
            done = 0
            for k_i in sorted(masked):
                o = sc_list[k_i] - 4 * tt
                j = j0 + k_i
                v.wait_ge(p._sems["MM"], mm_s[k_i])
                tile = pp[:, (j % 2) * NT:(j % 2 + 1) * NT]
                p.vinc(v.tensor_add(tile, tile,
                                    p.band_b[:, (3 - o) * 128:(3 - o) * 128 + 512]))
        assert p.n("VE") == ve_base + nmask

    for k_i in range(n_sc):
        j = j0 + k_i
        st["exp"][j] = exp_cnt[k_i]
        st["av"][j] = (mm_s[k_i + 1] + 1) if k_i + 1 < n_sc else (mm_s[n_sc - 1] + 2)

    @blk.scalar
    def _(s):
        for k_i in range(n_sc):
            j = j0 + k_i
            if k_i in masked:
                s.wait_ge(p._sems["VE"], ve_base + masked[k_i])
            else:
                s.wait_ge(p._sems["MM"], mm_s[k_i])
            if j >= 2:
                s.wait_ge(p._sems["MM"], st["av"][j - 2])
            par = j % 2
            p.sinc(s.activation(st["pt"][par], pp[:, par * NT:(par + 1) * NT],
                                AF.Exp, bias=negm8, scale=0.125))
    assert p.n("SC") == sc_base + n_sc
    st["j"] = j0 + n_sc
    mm_avl = p.n("MM")

    @blk.vector
    def _(v):
        v.wait_ge(p._sems["MM"], mm_avl)
        p.vstep(v, v.reciprocal(recl_f[:, 0:NT], psy[64:65, 0:NT]))
        p.vstep(v, v.tensor_copy(recl_b[:, 0:NT], recl_f[:, 0:NT]))
        if st.get("yun_ve"):
            v.wait_ge(p._sems["VE"], st["yun_ve"])
        p.vstep(v, v.tensor_copy(yun[0:64, 0:NT], psy[0:64, 0:NT]))
    ve_rc = p.n("VE")
    sc_yun = p.n("VE")

    for n in range(nh):
        @blk.tensor
        def _(t):
            t.wait_ge(p._sems["VE"], ve_rc)
            if st.get("bc_ve"):
                t.wait_ge(p._sems["VE"], st["bc_ve"])
            p.inc(t.matmul(psbc[:], p.ones_rbig[0:1, 0:64],
                           recl_b[:, n * 512:(n + 1) * 512], start=True, stop=True), "MM")
        mm_bc = p.n("MM")

        @blk.vector
        def _(v):
            v.wait_ge(p._sems["MM"], mm_bc)
            p.vstep(v, v.tensor_mul(Ydst[:, n * 512:(n + 1) * 512],
                                    yun[0:64, n * 512:(n + 1) * 512], psbc[0:64, :]))
        st["bc_ve"] = p.n("VE")
    st["yun_ve"] = p.n("VE")
    st[pkey] = p.n("VE")


def emit_attn_l0(p, d_wcd, d_wqu, d_wkv, d_wcp):
    blk = p.block
    NT = 512
    wcd = abf(p, 0, 4096)
    wqu = abf(p, 8192, 4096)
    wkv = abf(p, 16384, 512)
    latent = abf(p, 17408, 8192)
    kv = abf(p, 33792, 2048)
    qpair = [abf(p, 37888, 2048), abf(p, 41984, 2048)]
    qsq = abf(p, 46080, 2048)
    Y = abf(p, 50176, 16384)
    vhat = abf(p, 0, 16 * 65)
    ptile = [abf(p, 2080, 512), abf(p, 3104, 512)]
    yun = abf(p, 4128, 512)
    recl_f = af32(p, 5152, 512, parts=1)
    recl_b = abf(p, 16384, 512, parts=1)
    wcp = abf(p, 17408, 8192)
    ctx = ExitStack()
    pp = ctx.enter_context(p.nc.psum_tensor("pp_a0", [128, 1024], F32))
    psy = ctx.enter_context(p.nc.psum_tensor("psy_a0", [65, 2 * NT], F32))
    psbc = ctx.enter_context(p.nc.psum_tensor("psbc_a0", [64, 512], F32))
    ps_misc = ctx.enter_context(p.nc.psum_tensor("psm_a0", [128, 256], F32))

    @blk.gpsimd
    def _(g):
        _gload(p, g, [(_r3(wcd, 8), d_wcd), (_r3(wqu, 4), d_wqu),
                      (_r3(wkv, 4), d_wkv)],
               gate_mm=p.n("MM"), gate_all=True)
    ld_w = p.n("LD")

    evacs = {}
    jq = [0]

    def proj_group(nk, lhs_fn, dst, extra_waits=(), width=512, rows=128):
        j = jq[0]

        @blk.tensor
        def _(t):
            t.wait_ge(p._sems["LD"], ld_w)
            if j == 0:
                t.wait_ge(p._sems["VE"], p.xn_ve)
            if j >= 2:
                t.wait_ge(p._sems["VE"], evacs[j - 2])
            for sk, sv in extra_waits:
                t.wait_ge(p._sems[sk], sv)
            for c in range(nk):
                i1 = t.matmul(pp[0:rows, (j % 2) * 512:(j % 2) * 512 + width],
                              *lhs_fn(c), start=(c == 0), stop=(c == nk - 1))
            p.inc(i1, "MM")
        mm_g = p.n("MM")

        @blk.vector
        def _(v):
            v.wait_ge(p._sems["MM"], mm_g)
            p.vinc(v.tensor_copy(dst, pp[0:rows, (j % 2) * 512:(j % 2) * 512 + width]))
        evacs[j] = p.n("VE")
        jq[0] += 1
        return p.n("VE")

    for lc in range(4):
        for n in range(4):
            proj_group(8, (lambda lc, n: lambda dc: (
                wcd[:, dc * 512 + lc * 128: dc * 512 + lc * 128 + 128],
                p.xn_fm[:, dc * 2048 + n * 512: dc * 2048 + n * 512 + 512]))(lc, n),
                latent[:, lc * 2048 + n * 512: lc * 2048 + n * 512 + 512])
    sc_lat = p.n("VE")
    @blk.vector
    def _(v):
        p.vstep(v, v.memset(vhat[:], 1.0))
    ve_vh0 = p.n("VE")
    for n in range(4):
        proj_group(4, (lambda n: lambda lc: (
            wkv[:, lc * 128: lc * 128 + 64],
            latent[:, lc * 2048 + n * 512: lc * 2048 + n * 512 + 512]))(n),
            kv[0:64, n * 512:(n + 1) * 512], rows=64,
            extra_waits=[("VE", sc_lat)] if n == 0 else ())
    sc_k = p.n("VE")

    @blk.vector
    def _(v):
        v.wait_ge(p._sems["VE"], sc_k)
        p.vstep(v, v.tensor_copy(kv[64:128, :], kv[0:64, :]))
    ve_kcopy = p.n("VE")

    for tc in range(NTC):
        proj_group(4, (lambda tc: lambda lc: (
            latent[:, lc * 2048 + tc * 128: lc * 2048 + tc * 128 + 128],
            wkv[:, lc * 128 + 64: lc * 128 + 128]))(tc),
            vhat[:, tc * 65: tc * 65 + 64], width=64,
            extra_waits=[("VE", ve_vh0)] if tc == 0 else ())
    sc_kv = p.n("VE")
    ve_vhat = p.n("VE")

    st = {"j": 0, "exp": {}, "av": {}, "pt": ptile}
    pair_ve = {}
    last_q_mm = 0
    for g in range(8):
        qslot = qpair[g % 2]
        extra = []
        if g >= 2:
            extra.append(("VE", pair_ve[g - 2]))
        if g >= 1:
            extra.append(("SC", p.n("SC")))
            extra.append(("VE", p.n("VE")))
        for n in range(4):
            sc_q = proj_group(4, (lambda g, n: lambda lc: (
                wqu[:, lc * 1024 + g * 128: lc * 1024 + g * 128 + 128],
                latent[:, lc * 2048 + n * 512: lc * 2048 + n * 512 + 512]))(g, n),
                qslot[:, n * 512:(n + 1) * 512],
                extra_waits=extra if n == 0 else ())
        last_q_mm = p.n("MM")
        if g == 7:
            @blk.gpsimd
            def _(g_):
                _gload(p, g_, [(_r3(wcp, 8), d_wcp)],
                       gate_mm=last_q_mm, gate_all=False)
            p.ld_wcp = p.n("LD")
        for hh in range(2):
            h = 2 * g + hh
            kap = kv[64 * hh: 64 * hh + 64, :]
            _norm_bound(p, ps_misc, qslot[64 * hh: 64 * hh + 64, :], kv[0:64, :],
                        qsq, h, p.n("MM"))
            for tt in range(4):
                st["first_waits"] = {"VE": max(ve_vhat, p.nb_ve, ve_kcopy, sc_q)}
                _attn_head_tt(p, pp, psy[:, (tt % 2) * NT:(tt % 2 + 1) * NT], psbc,
                              qslot[64 * hh: 64 * hh + 64, :],
                              kap, vhat, h, tt, NT, True, yun,
                              Y[64 * hh: 64 * hh + 64,
                                g * 2048 + tt * NT: g * 2048 + (tt + 1) * NT],
                              recl_f, recl_b, st, pkey="psyf%d" % (tt % 2))
        pair_ve[g] = p.n("VE")
    ve_heads = p.n("VE")

    pove = {}
    npo = 0
    for tc in range(NTC):
        for fh in range(2):
            @blk.tensor
            def _(t):
                t.wait_ge(p._sems["LD"], p.ld_wcp)
                if npo == 0:
                    t.wait_ge(p._sems["VE"], ve_heads)
                    t.wait_ge(p._sems["SC"], p.n("SC"))
                if npo >= 2:
                    t.wait_ge(p._sems["VE"], pove[npo - 2])
                for dc in range(8):
                    i1 = t.matmul(pp[:, (npo % 2) * 512:(npo % 2) * 512 + 512],
                                  Y[:, dc * 2048 + tc * 128: dc * 2048 + tc * 128 + 128],
                                  wcp[:, dc * 1024 + fh * 512: dc * 1024 + fh * 512 + 512],
                                  start=(dc == 0), stop=(dc == 7))
                p.inc(i1, "MM")
            mm_pr = p.n("MM")

            @blk.vector
            def _(v):
                v.wait_ge(p._sems["MM"], mm_pr)
                xs = p.x_tm[:, tc * 1024 + fh * 512: tc * 1024 + fh * 512 + 512]
                pp_s = pp[:, (npo % 2) * 512:(npo % 2) * 512 + 512]
                p.xadd(v, tc * 2 + fh, lambda: v.tensor_add(xs, xs, pp_s))
            pove[npo] = p.n("VE")
            npo += 1
    p.attn0_mm = p.n("MM")
    ctx.close()


def emit_attn_l1(p, d_qkvw, d_qkvb, d_outw, d_outb):
    blk = p.block
    NT = 1024
    qkvw = [abf(p, 0, 3072), abf(p, 6144, 3072)]
    qt = abf(p, 12288, 2048)
    kt = abf(p, 16384, 2048)
    vhat2 = [abf(p, 24576, 16 * 65), abf(p, 26656, 16 * 65)]
    ptile = [abf(p, 28736, 1024), abf(p, 30784, 1024)]
    qsq = abf(p, 32832, 2048)
    yun = abf(p, 36928, 1024)
    recl_f = af32(p, 38976, 1024, parts=1)
    recl_b = abf(p, 43072, 1024, parts=1)
    qkvb_row = abf(p, 45120, 3072, parts=1)
    Y = abf(p, 51264, 16384)
    outb_row = abf(p, 84032, 1024, parts=1)
    outw = abf(p, 0, 8192)
    ctx = ExitStack()
    pp = ctx.enter_context(p.nc.psum_tensor("pp_a1", [128, 2048], F32))
    psy = ctx.enter_context(p.nc.psum_tensor("psy_a1", [65, NT], F32))
    misc2 = ctx.enter_context(p.nc.psum_tensor("psbc_a1", [128, 512], F32))
    psbc = misc2[0:64, 0:512]
    ps_misc = ctx.enter_context(p.nc.psum_tensor("psm_a1", [128, 256], F32))

    @blk.gpsimd
    def _(g):
        _gload(p, g, [(qkvb_row, d_qkvb), (outb_row, d_outb)],
               gate_mm=p.n("MM"), gate_all=True)
    ld_b = p.n("LD")

    def qkvsrc(gidx):
        def f(slot):
            return (_r3(slot, 8), d_qkvw[gidx])
        return f

    s_qkv = SegStream(p, "qkvw", qkvw, [qkvsrc(gi) for gi in range(8)])
    s_qkv.prime(gate_mm=p.n("MM"))

    st = {"j": 0, "exp": {}, "av": {}, "pt": ptile}
    jq = 0
    evq = {}
    for g in range(8):
        wbuf, wld = s_qkv.use(g)
        for c in range(2):
            for n in range(2):
                @blk.tensor
                def _(t):
                    t.wait_ge(p._sems["LD"], wld)
                    if g == 0 and c == 0 and n == 0:
                        t.wait_ge(p._sems["VE"], p.xn_ve)
                        t.wait_ge(p._sems["LD"], ld_b)
                    if jq >= 2:
                        t.wait_ge(p._sems["VE"], evq[jq - 2])
                    if st["j"] >= 1:
                        t.wait_ge(p._sems["SC"], st["exp"][st["j"] - 1])
                    for half in range(2):
                        for dc in range(8):
                            t.matmul(
                                pp[:, (jq % 2) * 1024 + half * 512:(jq % 2) * 1024 + half * 512 + 512],
                                wbuf[:, dc * 384 + c * 128: dc * 384 + c * 128 + 128],
                                p.xn_fm[:, dc * 2048 + n * 1024 + half * 512: dc * 2048 + n * 1024 + half * 512 + 512],
                                start=(dc == 0), stop=False)
                        i1 = t.matmul(
                            pp[:, (jq % 2) * 1024 + half * 512:(jq % 2) * 1024 + half * 512 + 512],
                            qkvb_row[:, g * 384 + c * 128: g * 384 + c * 128 + 128],
                            p.ones_rb[:], start=False, stop=True)
                    p.inc(i1, "MM")
                mm_q = p.n("MM")

                @blk.vector
                def _(v):
                    v.wait_ge(p._sems["MM"], mm_q)
                    if g > 0 and c == 0 and n == 0:
                        v.wait_ge(p._sems["MM"], p.pair_done_mm)
                    src = pp[:, (jq % 2) * 1024:(jq % 2) * 1024 + 1024]
                    ts_ = slice(n * 1024, (n + 1) * 1024)
                    if c == 0:
                        p.vinc(v.tensor_copy(qt[0:64, ts_], src[0:64, :]))
                        p.vinc(v.tensor_copy(kt[0:64, ts_], src[64:128, :]))
                    else:
                        p.vinc(v.tensor_copy(qt[64:128, ts_], src[0:64, :]))
                        p.vinc(v.tensor_copy(kt[64:128, ts_], src[64:128, :]))
                evq[jq] = p.n("VE")
                jq += 1
        for hh in range(2):
            @blk.vector
            def _(v):
                if g > 0:
                    v.wait_ge(p._sems["MM"], p.pair_done_mm)
                p.vstep(v, v.memset(vhat2[hh][:], 1.0))
        ve_vh0 = p.n("VE")
        for half in range(2):
            @blk.tensor
            def _(t):
                if jq >= 2:
                    t.wait_ge(p._sems["VE"], evq[jq - 2])
                if st["j"] >= 1:
                    t.wait_ge(p._sems["SC"], st["exp"][st["j"] - 1])
                for tci in range(8):
                    tc = half * 8 + tci
                    base = (jq % 2) * 1024 + tci * 128
                    for dc in range(8):
                        t.matmul(pp[:, base: base + 128],
                                 p.xn_fm[:, dc * 2048 + tc * 128: dc * 2048 + tc * 128 + 128],
                                 wbuf[:, dc * 384 + 256: dc * 384 + 384],
                                 start=(dc == 0), stop=False)
                    i1 = t.matmul(pp[:, base: base + 128],
                                  p.ones_rb[0:1, 0:128],
                                  qkvb_row[:, g * 384 + 256: g * 384 + 384],
                                  start=False, stop=True)
                p.inc(i1, "MM")
            mm_v = p.n("MM")

            @blk.vector
            def _(v):
                v.wait_ge(p._sems["MM"], mm_v)
                v.wait_ge(p._sems["VE"], ve_vh0)
                for tci in range(8):
                    tc = half * 8 + tci
                    base = (jq % 2) * 1024 + tci * 128
                    p.vinc(v.tensor_copy(vhat2[0][:, tc * 65: tc * 65 + 64],
                                         pp[:, base: base + 64]))
                    p.vinc(v.tensor_copy(vhat2[1][:, tc * 65: tc * 65 + 64],
                                         pp[:, base + 64: base + 128]))
            evq[jq] = p.n("VE")
            jq += 1
        s_qkv.done(g, p.n("MM"))
        for fn in getattr(p, "idx_scatters", [])[2 * g: 2 * g + 2]:
            fn()
        sc_pair = p.n("VE")
        ve_vhat = p.n("VE")

        for hh in range(2):
            h = 2 * g + hh
            qap = qt[64 * hh: 64 * hh + 64, :]
            kap = kt[64 * hh: 64 * hh + 64, :]
            _norm_bound(p, ps_misc, qap, kap, qsq, h, p.n("MM"))
            for tt in range(2):
                st["first_waits"] = {"VE": max(ve_vhat, p.nb_ve, sc_pair)}
                _attn_head_tt(p, pp, psy, psbc, qap, kap,
                              vhat2[hh], h, tt, NT, False, yun,
                              Y[64 * hh: 64 * hh + 64,
                                g * 2048 + tt * NT: g * 2048 + (tt + 1) * NT],
                              recl_f, recl_b, st)
        p.pair_done_mm = p.n("MM")
    ve_heads = p.n("VE")

    @blk.gpsimd
    def _(g_):
        _gload(p, g_, [(_r3(outw, 8), d_outw)],
               gate_mm=p.n("MM"), gate_all=True)
    ld_outw = p.n("LD")

    pove = {}
    npo = 0
    for tc in range(NTC):
        for fh in range(2):
            @blk.tensor
            def _(t):
                t.wait_ge(p._sems["LD"], ld_outw)
                if npo == 0:
                    t.wait_ge(p._sems["VE"], ve_heads)
                    t.wait_ge(p._sems["SC"], p.n("SC"))
                if npo >= 2:
                    t.wait_ge(p._sems["VE"], pove[npo - 2])
                for dc in range(8):
                    t.matmul(pp[:, (npo % 2) * 512:(npo % 2) * 512 + 512],
                             Y[:, dc * 2048 + tc * 128: dc * 2048 + tc * 128 + 128],
                             outw[:, dc * 1024 + fh * 512: dc * 1024 + fh * 512 + 512],
                             start=(dc == 0), stop=False)
                i1 = t.matmul(pp[:, (npo % 2) * 512:(npo % 2) * 512 + 512],
                              p.ones_rb[0:1, 0:128],
                              outb_row[:, fh * 512:(fh + 1) * 512],
                              start=False, stop=True)
                p.inc(i1, "MM")
            mm_pr = p.n("MM")

            @blk.vector
            def _(v):
                v.wait_ge(p._sems["MM"], mm_pr)
                xs = p.x_tm[:, tc * 1024 + fh * 512: tc * 1024 + fh * 512 + 512]
                pp_s = pp[:, (npo % 2) * 512:(npo % 2) * 512 + 512]
                p.xadd(v, tc * 2 + fh, lambda: v.tensor_add(xs, xs, pp_s))
            pove[npo] = p.n("VE")
            npo += 1
    p.attn1_mm = p.n("MM")
    ctx.close()



def _prep_host(I):
    H = {}
    H["x_sh"] = [np.ascontiguousarray(I["L0_attn_x"][c]) for c in range(8)]
    H["gwT0"] = np.ascontiguousarray(I["L0_ffn_gate_w"].T)
    H["b1fm"] = np.ascontiguousarray(
        I["L1_ffn_b1"].reshape(16, 16, 128).transpose(2, 0, 1).reshape(128, 256))
    H["qkvb"] = np.ascontiguousarray(I["L1_attn_qkv_b"].reshape(1, 3072))
    H["outb"] = np.ascontiguousarray(I["L1_attn_out_b"].reshape(1, 1024))
    H["rb"] = np.ascontiguousarray(I["L1_ffn_router_b"].reshape(1, 16))
    H["w1s"] = np.ascontiguousarray(
        I["L1_ffn_w1"].reshape(16, 8, 128, 4, 512).transpose(0, 2, 3, 1, 4))
    H["w2s"] = np.ascontiguousarray(
        I["L1_ffn_w2"].reshape(16, 16, 128, 2, 512).transpose(0, 2, 3, 1, 4))
    gus = I["L0_ffn_gate_up_proj"].reshape(8, 8, 128, 4, 512)
    H["gus"] = np.ascontiguousarray(gus[:, :, :, [0, 2, 1, 3], :].transpose(0, 2, 3, 1, 4))
    H["dns"] = np.ascontiguousarray(
        I["L0_ffn_down_proj"].reshape(8, 8, 128, 2, 512).transpose(0, 2, 3, 1, 4))
    q6 = I["L1_attn_qkv_w"].reshape(8, 128, 8, 6, 64)
    H["qkvs"] = np.ascontiguousarray(
        q6[:, :, :, [0, 1, 3, 4, 2, 5], :].transpose(2, 1, 0, 3, 4)
        .reshape(8, 128, 8, 384))
    b6 = I["L1_attn_qkv_b"].reshape(8, 6, 64)
    H["qkvb2"] = np.ascontiguousarray(b6[:, [0, 1, 3, 4, 2, 5], :].reshape(1, 3072))
    H["wcds"] = np.ascontiguousarray(I["L0_attn_Wc_down"].reshape(8, 128, 512).transpose(1, 0, 2))
    H["wqus"] = np.ascontiguousarray(I["L0_attn_Wq_up"].reshape(4, 128, 1024).transpose(1, 0, 2))
    H["wkvs"] = np.ascontiguousarray(I["L0_attn_Wkv_up"].reshape(4, 128, 128).transpose(1, 0, 2))
    H["wcps"] = np.ascontiguousarray(I["L0_attn_Wc_proj"].reshape(8, 128, 1024).transpose(1, 0, 2))
    H["outws"] = np.ascontiguousarray(I["L1_attn_out_w"].reshape(8, 128, 1024).transpose(1, 0, 2))
    return H


def _emit_l0(p, I, H, d_x):
    cd = host_consts()
    emit_consts(p, cd)
    emit_load_x(p, d_x)
    alloc_arena(p)
    alloc_ln(p)
    d_gwT = p.din("gwT0", H["gwT0"])
    d_wcd = p.din("wcd", H["wcds"])
    d_wqu = p.din("wqu", H["wqus"])
    d_wkv = p.din("wkv", H["wkvs"])
    d_wcp = p.din("wcp", H["wcps"])
    d_gup = p.din("gup", H["gus"])
    d_dp = p.din("dp", H["dns"])
    with p.nc.named_scope("LN0"):
        emit_ln_router(p, "L0", 8, 2, d_gwT, None)
    emit_barrier(p)
    with p.nc.named_scope("ATT0"):
        emit_attn_l0(p, d_wcd, d_wqu, d_wkv, d_wcp)
    emit_barrier(p)
    with p.nc.named_scope("FFN0"):
        emit_ffn_l0(p, d_gup, d_dp)


def _emit_l1(p, I, H):
    d_rw = p.din("rw", I["L1_ffn_router_w"])
    d_rb = p.din("rb", H["rb"])
    d_qkvw = p.din("qkvw", H["qkvs"])
    d_qkvb = p.din("qkvb", H["qkvb2"])
    d_outw = p.din("outw", H["outws"])
    d_outb = p.din("outb", H["outb"])
    d_w1 = p.din("w1", H["w1s"])
    d_b1fm = p.din("b1fm", H["b1fm"])
    d_w2 = p.din("w2", H["w2s"])
    d_b2 = p.din("b2", I["L1_ffn_b2"])
    with p.nc.named_scope("LN1"):
        emit_ln_router(p, "L1", 16, 4, d_rw, d_rb, sparse=True)
    with p.nc.named_scope("IDX1"):
        emit_idx_build(p)
    emit_barrier(p)
    with p.nc.named_scope("ATT1"):
        emit_attn_l1(p, d_qkvw, d_qkvb, d_outw, d_outb)
    emit_barrier(p)
    with p.nc.named_scope("FFN1"):
        emit_ffn_l1_sparse(p, d_w1, d_b1fm, d_w2, d_b2)


def build_layer0(I, H):
    p = Prog()
    d_x = p.din("x", H["x_sh"], sharded=True)
    d_x1 = p.dout("x1", [T, D])
    p.start()
    _emit_l0(p, I, H, d_x)
    emit_store_x(p, d_x1, p.x_tm)
    p.done()
    return p


def build_layer1(I, H, x1_sh):
    p = Prog()
    d_x = p.din("x", x1_sh, sharded=True)
    d_y = p.dout("y", [T, D])
    p.start()
    cd = host_consts()
    emit_consts(p, cd)
    emit_load_x(p, d_x)
    alloc_arena(p)
    alloc_ln(p)
    _emit_l1(p, I, H)
    emit_store_acc(p, d_y)
    p.done()
    return p


def build_fused(I, H):
    p = Prog()
    d_x = p.din("x", H["x_sh"], sharded=True)
    d_y = p.dout("y", [T, D])
    p.start()
    _emit_l0(p, I, H, d_x)
    emit_barrier(p)
    p.x_ready = [("VE", p.n("VE"))]
    _emit_l1(p, I, H)
    emit_store_acc(p, d_y)
    p.done()
    return p


FUSED = True


def kernel(**inputs):
    I = {k: np.asarray(v) for k, v in inputs.items()}
    H = _prep_host(I)
    if FUSED:
        p = build_fused(I, H)
        res = p.run()
        return np.stack([res.results[c]["y"] for c in range(8)], axis=0)
    p0 = build_layer0(I, H)
    r0 = p0.run()
    x1_sh = [np.ascontiguousarray(r0.results[c]["x1"]) for c in range(8)]
    p1 = build_layer1(I, H, x1_sh)
    r1 = p1.run()
    return np.stack([r1.results[c]["y"] for c in range(8)], axis=0)



I32 = mybir.dt.int32
CAP = 1152
NCS = CAP // 128
BIGV = 1.0e5


def _xr_f(p, off, n, parts=128):
    return p.x_tm[0:parts, off:off + n]


def _xr_b(p, off, n, parts=128):
    return p.x_tm[:].bitcast(BF16)[0:parts, 2 * off: 2 * off + n]


def emit_ffn_l1_sparse(p, d_w1, d_b1fm, d_w2, d_b2):
    blk = p.block
    E = 16
    b1_sb = af32(p, 88064, 16 * E)
    b2_sb = abf(p, 86016, 1024, parts=16)
    w1slots = [abf(p, 0, 4096), abf(p, 8192, 4096)]
    w2slots = [abf(p, 16384, 8192), abf(p, 32768, 8192)]
    hmid = abf(p, 49152, 16 * CAP)
    xg_tm = [_xr_b(p, 0, NCS * 1024), _xr_b(p, 0, NCS * 1024)]
    xg_fm2 = [_xr_b(p, 4608, 8 * CAP), _xr_b(p, 9216, 8 * CAP)]
    stage = [_xr_f(p, 13824, 1024), _xr_f(p, 14848, 1024)]
    idxf = _xr_f(p, 15872, 2 * NCS)
    cg = _xr_f(p, 15890, 2 * NCS)
    idxi = p.x_tm[:].bitcast(I32)[:, 15958:15976]
    cgxf = _xr_f(p, 16008, 1)
    cgxi = p.x_tm[:].bitcast(I32)[:, 16010:16011]
    p.ACC = p.nc.dram_tensor("ACC_l1", [T, D], F32, kind="Internal").ap()
    IDX = p.IDX

    ctx = ExitStack()
    ph = ctx.enter_context(p.nc.psum_tensor("ph_l1", [128, 2048], F32))
    po = ctx.enter_context(p.nc.psum_tensor("po_l1", [128, 1024], F32))
    pcb = ctx.enter_context(p.nc.psum_tensor("pcb_l1", [128, 512], F32))
    pos_ps = pcb[:, 0:16]
    off_ps = pcb[0:1, 128:144]

    @blk.sync
    def _(y):
        p.wait(y, "MM")
        p.inc(y.dma_start(out=b1_sb[:], in_=d_b1fm[:]), "LI", 16)
        y.wait_ge(p._sems["LI"], p.c["LI"])
    li_b1 = p.n("LI")

    @blk.gpsimd
    def _(g):
        _gload(p, g, [(b2_sb[:], d_b2[:])], gate_mm=p.n("MM"), gate_all=True)
    ld_b2 = p.n("LD")

    seq = list(range(E))
    s_w1 = SegStream(p, "w1", w1slots,
                     [(lambda e, q: lambda s: (_r3(s, 8), d_w1[e, :, q]))(e, q)
                      for e in seq for q in range(4)])
    s_w2 = SegStream(p, "w2", w2slots,
                     [(lambda e, h: lambda s: (_r3(s, 16), d_w2[e, :, h]))(e, h)
                      for e in seq for h in range(2)])
    s_w1.prime(gate_mm=p.n("MM"))
    s_w2.prime(gate_mm=p.n("MM"))

    cb_prev = 0
    for tc in range(NTC):
        for h in range(2):
            @blk.tensor
            def _(t):
                t.wait_ge(p._sems["VE"], p.xn_ve)
                t.wait_ge(p._sems["LD"], ld_b2)
                if cb_prev:
                    t.wait_ge(p._sems["VE"], cb_prev)
                p.inc(t.matmul(pcb[:], p.CT_sb[0:16, tc * 128:(tc + 1) * 128],
                               b2_sb[0:16, h * 512:(h + 1) * 512],
                               start=True, stop=True), "MM")
            mm_cb = p.n("MM")

            @blk.vector
            def _(v):
                v.wait_ge(p._sems["MM"], mm_cb)
                xs = p.x_tm[:, tc * 1024 + h * 512: tc * 1024 + (h + 1) * 512]
                p.xadd(v, tc * 2 + h, lambda: v.tensor_add(xs, xs, pcb[:]))
            cb_prev = p.n("VE")
    ve_cb2 = p.n("VE")

    @blk.gpsimd
    def _(g):
        g.wait_ge(p._sems["VE"], ve_cb2)
        for k in ("SC", "MM"):
            if p.c[k]:
                g.wait_ge(p._sems[k], p.c[k])
        dst = p.ACC.rearrange("(j p) d -> p j d", p=128)
        p.inc(g.dma_start(out=dst, in_=p.x_tm[:].rearrange("p (j d) -> p j d", j=NTC)),
              "LD", 16)
        g.wait_ge(p._sems["LD"], p.c["LD"])
    ld_init = p.n("LD")


    gath_ld = {}
    idxi_ve = {}
    tr_li = {}
    cgx_war = [0]
    w1_done_mm = {}
    w2_done_mm = {}

    def emit_gather(e):
        par = e % 2

        @blk.gpsimd
        def _(g):
            if e >= 1:
                g.wait_ge(p._sems["LI"], tr_li[e - 1])
            for cs in range(NCS):
                p.inc(g.dma_start(out=idxf[:, par * NCS + cs: par * NCS + cs + 1],
                                  in_=IDX[e * T + cs * 128: e * T + (cs + 1) * 128, 0:1]),
                      "LD", 16)
            g.wait_ge(p._sems["LD"], p.c["LD"])
        ld_i = p.n("LD")

        @blk.vector
        def _(v):
            v.wait_ge(p._sems["LD"], ld_i)
            p.vstep(v, v.tensor_copy(idxi[:, par * NCS: par * NCS + NCS],
                                     idxf[:, par * NCS: par * NCS + NCS]))
        idxi_ve[e] = p.n("VE")

        cflat = p.C_hbm
        for cs in range(NCS):
            @blk.vector
            def _(v):
                v.wait_ge(p._sems["LD"], ld_i)
                if cs == 0:
                    v.wait_ge(p._sems["VE"], p.c["VE"] - 0)
                if cgx_war[0]:
                    v.wait_ge(p._sems["LD"], cgx_war[0])
                p.vstep(v, v.tensor_scalar(cgxf, idxf[:, par * NCS + cs: par * NCS + cs + 1],
                                           16.0, float(e), ALU.mult, ALU.add))
                p.vstep(v, v.tensor_copy(cgxi, cgxf))
            ve_cgx = p.n("VE")

            @blk.gpsimd
            def _(g):
                g.wait_ge(p._sems["VE"], ve_cgx)
                ic = idxi[:, par * NCS + cs: par * NCS + cs + 1]
                p.inc(g.indirect_dma_start(
                    xg_tm[par][:, cs * 1024:(cs + 1) * 1024], None,
                    p.xn_hbm, IndirectOffsetOnAxis(ic, 0),
                    bounds_check=p.bc_t, oob_is_err=False), "LD", 16)
                p.inc(g.indirect_dma_start(
                    cg[:, par * NCS + cs: par * NCS + cs + 1], None,
                    cflat, IndirectOffsetOnAxis(cgxi, 0),
                    bounds_check=p.bc_et, oob_is_err=False), "LD", 16)
                g.wait_ge(p._sems["LD"], p.c["LD"])
            cgx_war[0] = p.n("LD")
        gath_ld[e] = p.n("LD")

    emit_gather(0)
    jg = 0
    npo = 0
    gel = {}
    pove = {}
    stage_war = {0: ld_init, 1: ld_init}
    for e in range(E):
        par = e % 2
        @blk.sync
        def _(y):
            y.wait_ge(p._sems["LD"], gath_ld[e])
            if e == 0:
                y.wait_ge(p._sems["LD"], ld_init)
            if e >= 2:
                y.wait_ge(p._sems["MM"], w1_done_mm[e - 2])
            for cs in range(NCS):
                for b in range(8):
                    i1 = y.dma_start(
                        out=xg_fm2[par][:, b * CAP + cs * 128: b * CAP + cs * 128 + 128],
                        in_=xg_tm[par][:, cs * 1024 + b * 128: cs * 1024 + b * 128 + 128],
                        transpose=True)
                    p.inc(i1, "LI", 16)
            y.wait_ge(p._sems["LI"], p.c["LI"])
        tr_li[e] = p.n("LI")
        if e + 1 < E:
            emit_gather(e + 1)

        jg0 = jg
        sc0 = p.n("SC")
        mm_as = {}
        subt = ((0, 512), (512, 512), (1024, 128))
        sub_of = {}
        for ic in range(16):
            q = ic // 4
            w1buf, w1ld = s_w1.use(e * 4 + q)

            @blk.tensor
            def _(t):
                t.wait_ge(p._sems["LD"], w1ld)
                if ic == 0:
                    t.wait_ge(p._sems["LI"], tr_li[e])
                for si, (n0, nw) in enumerate(subt):
                    j = jg0 + ic * 3 + si
                    if j >= 4:
                        t.wait_ge(p._sems["SC"], gel[j - 4])
                    for dc in range(8):
                        i1 = t.matmul(
                            ph[:, (j % 4) * 512:(j % 4) * 512 + nw],
                            w1buf[:, dc * 512 + (ic % 4) * 128: dc * 512 + (ic % 4) * 128 + 128],
                            xg_fm2[par][:, dc * CAP + n0: dc * CAP + n0 + nw],
                            start=(dc == 0), stop=(dc == 7))
                    p.inc(i1, "MM")
                    mm_as[(ic, si)] = p.n("MM")
                    sub_of[j] = (ic, si, n0, nw)
                    gel[j] = sc0 + ic * 3 + si + 1
            if ic % 4 == 3:
                s_w1.done(e * 4 + q, p.n("MM"))
        jg = jg0 + 48
        w1_done_mm[e] = p.n("MM")

        @blk.scalar
        def _(s):
            for j in range(jg0, jg):
                ic, si, n0, nw = sub_of[j]
                s.wait_ge(p._sems["MM"], mm_as[(ic, si)])
                if j == jg0:
                    s.wait_ge(p._sems["LI"], li_b1)
                    if e >= 1:
                        s.wait_ge(p._sems["MM"], w2_done_mm[e - 1])
                p.sinc(s.activation(hmid[:, ic * CAP + n0: ic * CAP + n0 + nw],
                                    ph[:, (j % 4) * 512:(j % 4) * 512 + nw],
                                    AF.Gelu, bias=b1_sb[:, e * 16 + ic: e * 16 + ic + 1]))
        assert p.n("SC") == sc0 + 48
        sc_a_done = p.n("SC")

        for cs in range(NCS):
            for h in range(2):
                w2buf, w2ld = s_w2.use(e * 2 + h)

                @blk.tensor
                def _(t):
                    t.wait_ge(p._sems["LD"], w2ld)
                    if cs == 0 and h == 0:
                        t.wait_ge(p._sems["SC"], sc_a_done)
                    if npo >= 2:
                        t.wait_ge(p._sems["VE"], pove[npo - 2])
                    for ic in range(16):
                        i1 = t.matmul(po[:, (npo % 2) * 512:(npo % 2) * 512 + 512],
                                      hmid[:, ic * CAP + cs * 128: ic * CAP + cs * 128 + 128],
                                      w2buf[:, ic * 512:(ic + 1) * 512],
                                      start=(ic == 0), stop=(ic == 15))
                    p.inc(i1, "MM")
                mm_b = p.n("MM")

                @blk.vector
                def _(v):
                    v.wait_ge(p._sems["MM"], mm_b)
                    if h == 0 and stage_war[cs % 2]:
                        v.wait_ge(p._sems["LD"], stage_war[cs % 2])
                    p.vstep(v, v.tensor_scalar(
                        stage[cs % 2][:, h * 512:(h + 1) * 512],
                        po[:, (npo % 2) * 512:(npo % 2) * 512 + 512],
                        cg[:, par * NCS + cs: par * NCS + cs + 1], None, ALU.mult))
                pove[npo] = p.n("VE")
                npo += 1
            ve_st = p.n("VE")

            @blk.gpsimd
            def _(g):
                g.wait_ge(p._sems["VE"], ve_st)
                p.inc(g.indirect_dma_start(
                    p.ACC, IndirectOffsetOnAxis(idxi[:, par * NCS + cs: par * NCS + cs + 1], 0),
                    stage[cs % 2][:], None, compute_op=ALU.add,
                    bounds_check=p.bc_t, oob_is_err=False), "LD", 16)
                g.wait_ge(p._sems["LD"], p.c["LD"])
            stage_war[cs % 2] = p.n("LD")
        w2_done_mm[e] = p.n("MM")
        s_w2.done(e * 2, p.n("MM"))
        s_w2.done(e * 2 + 1, p.n("MM"))
    p.ffn1_mm = p.n("MM")
    ctx.close()


def emit_store_acc(p, d_y):
    blk = p.block

    @blk.sync
    def _(y):
        for k in ("VE", "SC", "MM", "LD", "LI", "LC", "LX"):
            if p.c[k]:
                p.wait(y, k)
        p.inc(y.dma_start(out=d_y, in_=p.ACC), "OUT", 16)
        p.wait(y, "OUT")


def emit_idx_build(p):
    blk = p.block
    E = 16
    big = p.arena[:].bitcast(F32)[:, 0:256]
    offrow = p.arena[:].bitcast(F32)[0:1, 256:512]
    atmp = p.arena[:].bitcast(F32)[:, 256:272]
    posm = p.arena[:].bitcast(F32)[:, 272:288]
    posi_all = p.arena[:].bitcast(I32)[:, 22272:22528]
    p.IDX = p.nc.dram_tensor("IDX_l1", [E * T, 1], F32, kind="Internal").ap()
    ctx = ExitStack()
    pall = ctx.enter_context(p.nc.psum_tensor("p_idx", [128, 256], F32))
    pcnt = ctx.enter_context(p.nc.psum_tensor("p_cnt", [1, 256], F32))

    @blk.vector
    def _(v):
        p.wait(v, "VE")
        p.vstep(v, v.memset(big, BIGV))
    ve_big = p.n("VE")

    @blk.gpsimd
    def _(g):
        g.wait_ge(p._sems["VE"], ve_big)
        for k in ("SC", "MM"):
            if p.c[k]:
                g.wait_ge(p._sems[k], p.c[k])
        p.inc(g.dma_start(out=p.IDX.rearrange("(a n) o -> a (n o)", a=128), in_=big),
              "LD", 16)
        g.wait_ge(p._sems["LD"], p.c["LD"])
    ld_init = p.n("LD")

    @blk.tensor
    def _(t):
        t.wait_ge(p._sems["VE"], p.xn_ve)
        t.wait_ge(p._sems["LC"], p.consts_lc)
        for tc in range(NTC):
            i1 = t.matmul(pcnt[:, tc * 16:(tc + 1) * 16], p.ones_cf[:],
                          p.sel_all[:, tc * 16:(tc + 1) * 16], start=True, stop=True)
        p.inc(i1, "MM")
    mm_cnt = p.n("MM")

    @blk.vector
    def _(v):
        v.wait_ge(p._sems["MM"], mm_cnt)
        v.wait_ge(p._sems["VE"], ve_big)
        p.vstep(v, v.tensor_copy(offrow[0:1, 0:16], p.erow_sb[:]))
        for tc in range(1, NTC):
            p.vstep(v, v.scalar_tensor_tensor(
                offrow[0:1, tc * 16:(tc + 1) * 16],
                pcnt[0:1, (tc - 1) * 16: tc * 16], 1.0,
                offrow[0:1, (tc - 1) * 16: tc * 16], ALU.mult, ALU.add))
    ve_off = p.n("VE")

    @blk.tensor
    def _(t):
        t.wait_ge(p._sems["VE"], ve_off)
        for tc in range(NTC):
            t.matmul(pall[:, tc * 16:(tc + 1) * 16], p.ut_sb[:],
                     p.sel_all[:, tc * 16:(tc + 1) * 16], start=True, stop=False)
            i1 = t.matmul(pall[:, tc * 16:(tc + 1) * 16], p.ones_rf[:],
                          offrow[0:1, tc * 16:(tc + 1) * 16], start=False, stop=True)
        p.inc(i1, "MM")
    mm_pos = p.n("MM")

    @blk.vector
    def _(v):
        v.wait_ge(p._sems["MM"], mm_pos)
        for tc in range(NTC):
            p.vstep(v, v.tensor_scalar(atmp, p.sel_all[:, tc * 16:(tc + 1) * 16],
                                       -BIGV, BIGV, ALU.mult, ALU.add))
            p.vstep(v, v.tensor_add(posm, atmp, pall[:, tc * 16:(tc + 1) * 16]))
            p.vstep(v, v.tensor_copy(posi_all[:, tc * 16:(tc + 1) * 16], posm))
    ve_pos = p.n("VE")

    @blk.gpsimd
    def _(g):
        g.wait_ge(p._sems["VE"], ve_pos)
        g.wait_ge(p._sems["LD"], ld_init)
        p.bc_et = g.alloc_register("bc_et")
        g.reg_mov(p.bc_et, E * T - 1)
        p.bc_t = g.alloc_register("bc_t")
        g.reg_mov(p.bc_t, T - 1)

    def mk(tc):
        def emit():
            @blk.gpsimd
            def _(g):
                for e in range(16):
                    p.inc(g.indirect_dma_start(
                        p.IDX,
                        IndirectOffsetOnAxis(posi_all[:, tc * 16 + e: tc * 16 + e + 1], 0),
                        p.iota16_sb[:, tc: tc + 1], None, compute_op=ALU.bypass,
                        bounds_check=p.bc_et, oob_is_err=False), "LD", 16)
                g.wait_ge(p._sems["LD"], p.c["LD"])
        return emit
    p.idx_scatters = [mk(tc) for tc in range(NTC)]
    ctx.close()


def emit_store_acc(p, d_y):
    blk = p.block

    @blk.sync
    def _(y):
        for k in ("VE", "SC", "MM", "LD", "LI", "LC", "LX"):
            if p.c[k]:
                p.wait(y, k)
        p.inc(y.dma_start(out=d_y, in_=p.ACC), "OUT", 16)
        p.wait(y, "OUT")
